# version 11 (final)
# baseline (speedup 1.0000x reference)
import numpy as np
import ml_dtypes
import concourse.bass as bass
import concourse.mybir as mybir
from concourse.bass_utils import run_bass_kernel_spmd

F32 = mybir.dt.float32
BF16 = mybir.dt.bfloat16
I8 = mybir.dt.int8
AF = mybir.ActivationFunctionType
ALU = mybir.AluOpType
AX = mybir.AxisListType

P = 128
NT = 16
T = 2048
D = 1024
DQ = 3072
FF = 4096
NCORE = 8
GRP = 4
NGRP = NT // GRP
NC1 = 8
NC2 = 32
INV_D = 1.0 / D
INV_FF = 1.0 / FF
INV_NW = 1.0 / (4096.0 * 1024.0)
LN_EPS = 1e-5
ENGINES = ("sync", "tensor", "vector", "scalar", "gpsimd")


class Sem:
    __slots__ = ("h", "n")

    def __init__(self, h):
        self.h = h
        self.n = 0


class Bld:
    def __init__(self, nc):
        self.nc = nc
        self.q = {e: [] for e in ENGINES}

    def op(self, eng, fn, wait=(), inc=None, inc_k=1):
        waits = [(s.h, v) for (s, v) in wait if v]
        tgt = None
        if inc is not None:
            inc.n += inc_k
            tgt = (inc.h, inc_k)

        def f(e):
            for h, v in waits:
                e.wait_ge(h, v)
            ins = fn(e)
            if tgt is not None:
                ins.then_inc(*tgt)
        self.q[eng].append(f)
        return inc.n if inc is not None else None

    def flush(self):
        with self.nc.Block() as block:
            for e in ENGINES:
                ops = self.q[e]
                if not ops:
                    continue

                def make_body(ops):
                    def body(engine):
                        for f in ops:
                            f(engine)
                    return body
                getattr(block, e)(make_body(ops))
        self.q = {e: [] for e in ENGINES}

    def barrier(self, s_ph, dmas):
        for e in ENGINES:
            waits = [(dmas[e].h, dmas[e].n)] if e in dmas else []

            def f(engine, waits=tuple(waits), h=s_ph.h):
                for hh, v in waits:
                    engine.wait_ge(hh, v)
                engine.drain()
                engine.nop().then_inc(h, 1)
            self.q[e].append(f)
        s_ph.n += 5
        val = s_ph.n
        for e in ENGINES:
            def g(engine, h=s_ph.h, v=val):
                engine.wait_ge(h, v)
            self.q[e].append(g)


def build_nc(debug=False):
    nc = bass.Bass("TRN2", target_bir_lowering=False)
    dbg_reg = {}

    x_in = nc.dram_tensor("x_in", [T, D], F32, kind="ExternalInput")
    wqkv_d = nc.dram_tensor("wqkv_cols", [24 * 128, 1024], F32, kind="ExternalInput")
    wgate_d = nc.dram_tensor("wgateT", [D, D], F32, kind="ExternalInput")
    wout_d = nc.dram_tensor("woutT", [D, D], F32, kind="ExternalInput")
    w1sh_d = nc.dram_tensor("w1_shard", [512, D], F32, kind="ExternalInput")
    w2sh_d = nc.dram_tensor("w2t_shard", [512, D], F32, kind="ExternalInput")
    bgfm_d = nc.dram_tensor("bgate_fm", [P, 8], F32, kind="ExternalInput")
    b1fm_d = nc.dram_tensor("b1_fm", [P, 32], F32, kind="ExternalInput")
    b2row_d = nc.dram_tensor("b2_row", [1, D], F32, kind="ExternalInput")
    g1row_d = nc.dram_tensor("g1_row", [1, D], F32, kind="ExternalInput")
    b1lrow_d = nc.dram_tensor("b1l_row", [1, D], F32, kind="ExternalInput")
    g2row_d = nc.dram_tensor("g2_row", [1, FF], F32, kind="ExternalInput")
    b2lrow_d = nc.dram_tensor("b2l_row", [1, FF], F32, kind="ExternalInput")
    parfm_d = nc.dram_tensor("param_fm", [P, 8], F32, kind="ExternalInput")
    sel8_d = nc.dram_tensor("sel8", [8, 1], F32, kind="ExternalInput")
    iota_d = nc.dram_tensor("iota_bc", [P, T], F32, kind="ExternalInput")
    hs16_d = nc.dram_tensor("hs16_in", [P, 128], F32, kind="ExternalInput")
    hs16t_d = nc.dram_tensor("hs16t_in", [16, 1024], F32, kind="ExternalInput")
    y_out = nc.dram_tensor("y_out", [T, D], F32, kind="ExternalOutput")

    cc_m_in = nc.dram_tensor("cc_m_in", [1, 2], F32)
    cc_m_out = nc.dram_tensor("cc_m_out", [1, 2], F32)
    cc_w1_in = nc.dram_tensor("cc_w1_in", [512, 1024], BF16)
    cc_w1_out = nc.dram_tensor("cc_w1_out", [4096, 1024], BF16)
    cc_w2_in = nc.dram_tensor("cc_w2_in", [512, 1024], BF16)
    cc_w2_out = nc.dram_tensor("cc_w2_out", [4096, 1024], BF16)
    cc_s_in = nc.dram_tensor("cc_s_in", [P, 16], F32)
    cc_s_out = nc.dram_tensor("cc_s_out", [1024, 16], F32)
    sin_dram = nc.dram_tensor("sin_dram", [1, T], F32)
    af_dram = nc.dram_tensor("af_dram", [T, D], BF16)

    import contextlib
    stack = contextlib.ExitStack()
    with stack:
        mksem = lambda n: Sem(stack.enter_context(nc.semaphore(n)))
        s_sy = mksem("s_sy")
        s_sc = mksem("s_sc")
        s_gd = mksem("s_gd")
        s_cc = mksem("s_cc")
        s_t = mksem("s_t")
        s_v = mksem("s_v")
        s_a = mksem("s_a")
        s_p = mksem("s_p")
        s_ph = mksem("s_ph")
        DMAS = {"sync": s_sy, "scalar": s_sc, "gpsimd": s_gd}

        sbt = lambda n, s, dt: stack.enter_context(nc.sbuf_tensor(n, s, dt))

        A1 = sbt("A1", [P, 8192], BF16)
        A2 = sbt("A2", [P, 16384], BF16)
        A3 = sbt("A3", [P, 16384], BF16)
        A4 = sbt("A4", [P, 16384], BF16)
        A5 = sbt("A5", [P, 16384], BF16)
        A6 = sbt("A6", [P, 8192], BF16)
        A7 = sbt("A7", [P, 8192], BF16)
        A8 = sbt("A8", [P, 4096], BF16)
        A9 = sbt("A9", [P, 4096], BF16)
        XS = sbt("XS", [P, 1024], F32)
        gapv = sbt("gapv", [P, 256], BF16)
        gapa = sbt("gapa", [P, 256], BF16)

        SM = sbt("SM", [P, 3328], BF16)
        SMf = SM.bitcast(F32)
        mq_sb = SMf[:, 0:2]
        sw_sb = SMf[:, 2:4]
        wcol = SMf[:, 4:6]
        bgfm_sb = SMf[:, 8:16]
        dpar_sb = SMf[:, 16:24]
        lnd_sb = SMf[:, 24:32]
        b1fm_sb = SMf[:, 32:64]
        stat = SMf[:, 64:192].rearrange("p (a b) -> p a b", b=NT)
        st2 = SMf[:, 192:320].rearrange("p (a b) -> p a b", b=NT)
        sinfm_sb = SMf[:, 320:336]
        sout_sb = SMf[:, 336:352]
        wtot = SMf[0:1, 352:354]
        ones1 = SM[0:1, 768:896]
        sel_sb = SM[0:8, 896:897]
        hs16_sb = SM[:, 1024:1152].rearrange("p (a b) -> p a b", b=NT)
        hs16t_sb = SM[0:16, 1152:2176].rearrange("p (a b) -> p a b", b=128)
        b2row_sb = SM[0:1, 2176:3200]
        recipb_sb = sbt("recipb_sb", [16, T], BF16)

        xq1g = A2.bitcast(I8)[:, 28672:32768].rearrange("p (j d) -> p j d", d=D)
        PS1 = stack.enter_context(nc.psum_tensor("PS1", [P, 2048], F32))
        PS2 = stack.enter_context(nc.psum_tensor("PS2", [P, 2048], F32))

        b = Bld(nc)

        def gdma(dst, src, wait=()):
            return b.op("gpsimd", lambda e, d=dst, s=src: e.dma_start(d, s), wait=wait,
                        inc=s_gd, inc_k=16)

        def sdma(dst, src, wait=()):
            return b.op("sync", lambda e, d=dst, s=src: e.dma_start(d, s), wait=wait,
                        inc=s_sy, inc_k=16)

        def cdma(dst, src, wait=()):
            return b.op("scalar", lambda e, d=dst, s=src: e.dma_start(d, s), wait=wait,
                        inc=s_sc, inc_k=16)

        def strp(dst, src, wait=()):
            return b.op("sync", lambda e, d=dst, s=src: e.dma_start_transpose(d, s),
                        wait=wait, inc=s_sy, inc_k=16)

        def vop(fn, wait=(), inc=None):
            b.op("vector", fn, wait=wait)
            return b.op("vector", lambda e: e.tensor_copy(gapv[:, :], gapv[:, :]), inc=inc)

        def aop(fn, wait=(), inc=None):
            b.op("scalar", fn, wait=wait)
            return b.op("scalar", lambda e: e.copy(gapa[:, :], gapa[:, :]), inc=inc)

        def dbg(name, ap, wait=()):
            if not debug:
                return
            shape = list(ap.shape)
            dt = ap.dtype
            d = nc.dram_tensor("dbg_" + name, shape, dt, kind="ExternalOutput")
            idx = [slice(None)] * len(shape)
            b.op("sync", lambda e, d=d, s=ap: e.dma_start(d[tuple(idx)], s),
                 wait=wait, inc=s_sy, inc_k=16)

        wsh1 = A4.bitcast(F32)[:, 0:4096].rearrange("p (c n) -> p c n", n=D)
        wsh2 = A5.bitcast(F32)[:, 0:4096].rearrange("p (c n) -> p c n", n=D)
        tthr = A2.bitcast(F32)
        ti8 = A3.bitcast(I8)
        tern1 = A6[:, 0:4096].rearrange("p (c n) -> p c n", n=D)
        tern2 = A7[:, 0:4096].rearrange("p (c n) -> p c n", n=D)
        ttr = A8[:, 0:2048].rearrange("p (a k q) -> p a k q", a=2, k=8)
        wabs = A9

        gdma(hs16_sb[:, :, :], hs16_d.rearrange("p (c m) -> p c m", m=16))
        gdma(hs16t_sb[:, :, :], hs16t_d.rearrange("m (c p) -> m c p", p=128))
        gdma(sel_sb[:, :], sel8_d[:, :])
        gdma(b2row_sb[:, :], b2row_d[:, :])
        b.op("gpsimd", lambda e: e.memset(ones1[:, :], 1.0), inc=s_gd, inc_k=1)
        sdma(bgfm_sb[:, :], bgfm_d[:, :])
        sdma(b1fm_sb[:, :], b1fm_d[:, :])
        t_par = sdma(dpar_sb[:, :], parfm_d[:, :])
        sdma(wsh1[:, :, :], w1sh_d.rearrange("(c p) n -> p c n", p=128))
        t_wsh = sdma(wsh2[:, :, :], w2sh_d.rearrange("(c p) n -> p c n", p=128))

        t_sig = aop(lambda e: e.activation(dpar_sb[:, :], dpar_sb[:, :], AF.Sigmoid),
                    wait=[(s_sy, t_par)], inc=s_a)
        vop(lambda e: e.tensor_scalar_max(dpar_sb[:, :], dpar_sb[:, :], 1e-8),
            wait=[(s_a, t_sig)])
        t_dcp = vop(lambda e: e.tensor_copy(lnd_sb[:, :], dpar_sb[:, :]), inc=s_v)
        t_lnd = aop(lambda e: e.activation(lnd_sb[:, :], lnd_sb[:, :], AF.Ln),
                    wait=[(s_v, t_dcp)], inc=s_a)

        b.op("vector", lambda e: e.reduce_sum(
            wcol[:, 0:1], wsh1.rearrange("p c n -> p (c n)")[:, :], AX.X,
            apply_absolute_value=True), wait=[(s_sy, t_wsh)], inc=s_v)
        t_abs = b.op("vector", lambda e: e.reduce_sum(
            wcol[:, 1:2], wsh2.rearrange("p c n -> p (c n)")[:, :], AX.X,
            apply_absolute_value=True), inc=s_v)
        t_wt = b.op("gpsimd", lambda e: e.reduce_sum(wtot[:, :], wcol[:, :], AX.C),
                    wait=[(s_v, t_abs)], inc=s_p)
        t_mup = sdma(cc_m_in[:, :], wtot[:, :], wait=[(s_p, t_wt)])
        t_mar = b.op("gpsimd", lambda e: e.collective_compute(
            "AllReduce", ALU.add, replica_groups=[list(range(NCORE))],
            ins=[cc_m_in.ap().opt()], outs=[cc_m_out.ap().opt()]),
            wait=[(s_sy, t_mup)], inc=s_cc)
        t_mbc = sdma(mq_sb[:, :], cc_m_out[:, :].broadcast_to([P, 2]), wait=[(s_cc, t_mar)])
        vop(lambda e: e.tensor_scalar(mq_sb[:, :], mq_sb[:, :], INV_NW, 1e-5,
                                      ALU.mult, ALU.max), wait=[(s_sy, t_mbc)])
        vop(lambda e: e.reciprocal(sw_sb[:, :], mq_sb[:, :]), inc=s_v)
        tern_tok = []
        for j in range(2):
            wsh = (wsh1, wsh2)[j]
            tern = (tern1, tern2)[j]
            b.op("vector", lambda e, wsh=wsh, j=j: e.tensor_scalar_mul(
                tthr[:, :4096], wsh.rearrange("p c n -> p (c n)")[:, :],
                sw_sb[:, j:j + 1]), inc=s_v)
            b.op("vector", lambda e: e.tensor_scalar_mul(ti8[:, :4096], tthr[:, :4096],
                                                         1.0), inc=s_v)
            b.op("vector", lambda e: e.tensor_scalar_max(ti8[:, :4096], ti8[:, :4096],
                                                         -1.0), inc=s_v)
            b.op("vector", lambda e: e.tensor_scalar_min(ti8[:, :4096], ti8[:, :4096],
                                                         1.0), inc=s_v)
            tern_tok.append(b.op("vector", lambda e, tern=tern: e.tensor_copy(
                tern.rearrange("p c n -> p (c n)")[:, :], ti8[:, :4096]), inc=s_v))

        for o4 in range(4):
            tt = strp(ttr[:, o4 % 2, :, :], tern1[:, o4, :], wait=[(s_v, tern_tok[0])])
            sdma(cc_w1_in[o4 * 128:(o4 + 1) * 128, :].rearrange("p (k q) -> p k q", q=128),
                 ttr[:, o4 % 2, :, :], wait=[(s_sy, tt)])
        sdma(cc_w2_in.rearrange("(c p) n -> p c n", p=128), tern2[:, :, :],
             wait=[(s_v, tern_tok[1])])
        t_wcon = s_sy.n
        t_ag1 = b.op("gpsimd", lambda e: e.collective_compute(
            "AllGather", ALU.bypass, replica_groups=[list(range(NCORE))],
            ins=[cc_w1_in.ap().opt()], outs=[cc_w1_out.ap().opt()]),
            wait=[(s_sy, t_wcon)], inc=s_cc)
        t_ag2 = b.op("gpsimd", lambda e: e.collective_compute(
            "AllGather", ALU.bypass, replica_groups=[list(range(NCORE))],
            ins=[cc_w2_in.ap().opt()], outs=[cc_w2_out.ap().opt()]), inc=s_cc)

        b.barrier(s_ph, DMAS)

        xnT = A1.rearrange("p (k q) -> p k q", q=1024)
        qp_fm = A2.rearrange("p (c t) -> p c t", t=T)
        v_fm = A3.rearrange("p (c t) -> p c t", t=T)
        kv_fm = A4.rearrange("p (c t) -> p c t", t=T)
        k_fm = A5.rearrange("p (c t) -> p c t", t=T)
        wq_s = A6[:, 0:4096].rearrange("p (r k q) -> p r k q", r=4, k=8)
        xsv = XS
        xn_t = A8.rearrange("p (r d) -> p r d", d=D)
        a9f = A9.bitcast(F32)
        phi_f = a9f.rearrange("p (r q) -> p r q", q=1024)
        phi_e = A8.rearrange("p (r q) -> p r q", q=1024)

        CHUNKS = [("q", c) for c in range(8)]
        for c in range(8):
            CHUNKS.append(("k", c))
            CHUNKS.append(("v", c))
        ps_free = [0, 0]
        wq_busy = [0, 0, 0, 0]
        xnT_busy = [0] * 8
        xs_free = [(None, 0)]
        xn_free = [0, 0]
        phi_busy = [0, 0]
        tkp = [0] * 8

        for h in range(2):
            for ih in range(8):
                i = h * 8 + ih
                sl = ih % 2
                t_x = sdma(xsv[:, :], x_in.rearrange("(i p) d -> p i d", p=128)[:, i, :],
                           wait=[xs_free[0]] if xs_free[0][0] else [])
                vop(lambda e, i=i: e.reduce_sum(
                    stat[:, 0, i:i + 1], xsv[:, :], AX.X), wait=[(s_sy, t_x)])
                t_sq = b.op("scalar", lambda e, i=i: e.activation(
                    PS2[:, 1024:2048], xsv[:, :], AF.Square,
                    accum_out=stat[:, 1, i:i + 1]),
                    wait=[(s_sy, t_x)], inc=s_a)
                t_neg = vop(lambda e, i=i: e.tensor_scalar_mul(
                    stat[:, 2, i:i + 1], stat[:, 0, i:i + 1], -INV_D),
                    wait=[(s_a, t_sq)], inc=s_v)
                t_mu2 = aop(lambda e, i=i: e.activation(
                    stat[:, 4, i:i + 1], stat[:, 2, i:i + 1], AF.Square),
                    wait=[(s_v, t_neg)], inc=s_a)
                vop(lambda e, i=i: e.tensor_scalar(
                    stat[:, 3, i:i + 1], stat[:, 1, i:i + 1], INV_D, float(LN_EPS),
                    ALU.mult, ALU.add))
                t_var = vop(lambda e, i=i: e.tensor_tensor(
                    stat[:, 3, i:i + 1], stat[:, 3, i:i + 1], stat[:, 4, i:i + 1],
                    ALU.subtract), wait=[(s_a, t_mu2)], inc=s_v)
                t_std = aop(lambda e, i=i: e.activation(
                    stat[:, 3, i:i + 1], stat[:, 3, i:i + 1], AF.Sqrt),
                    wait=[(s_v, t_var)], inc=s_a)
                t_rst = vop(lambda e, i=i: e.reciprocal(
                    stat[:, 3, i:i + 1], stat[:, 3, i:i + 1]), wait=[(s_a, t_std)],
                    inc=s_v)
                t_xn = b.op("vector", lambda e, i=i, sl=sl: e.tensor_scalar(
                    xn_t[:, sl, :], xsv[:, :], stat[:, 2, i:i + 1], stat[:, 3, i:i + 1],
                    ALU.add, ALU.mult),
                    wait=[(s_sy, xn_free[sl]), (s_v, t_rst)], inc=s_v)
                xs_free[0] = (s_v, t_xn)
                xn_free[sl] = strp(xnT[:, :, 128 * ih:128 * (ih + 1)], xn_t[:, sl, :],
                                   wait=[(s_v, t_xn), (s_t, xnT_busy[ih])])
            t_xnT_half = s_sy.n

            wtoks = {}
            def issue_wq(si):
                wsl = si % 4
                wtoks[si] = gdma(wq_s[:, wsl, :, :],
                                 wqkv_d[128 * si:128 * (si + 1), :]
                                 .rearrange("p (k m) -> p k m", m=128),
                                 wait=[(s_t, wq_busy[wsl])])
            for si in range(4):
                issue_wq(si)
            for si, (kind, fc) in enumerate(CHUNKS):
                wsl = si % 4
                wsem = s_gd
                t_w = wtoks[si]
                psl = si % 2
                PSX = (PS1, PS2)[psl]
                for nsl in range(2):
                    for kc in range(8):
                        tok = b.op("tensor", lambda e, kc=kc, nsl=nsl, wsl=wsl, ih=0,
                                   PSX=PSX:
                                   e.matmul(PSX[:, 512 * nsl:512 * (nsl + 1)],
                                            wq_s[:, wsl, kc, :],
                                            xnT[:, kc, 512 * nsl:512 * (nsl + 1)],
                                            start=(kc == 0), stop=(kc == 7)),
                                   wait=([(wsem, t_w), (s_sy, t_xnT_half),
                                          (s_v, ps_free[psl])]
                                         if (nsl == 0 and kc == 0) else []),
                                   inc=(s_t if (nsl == 1 and kc == 7) else None))
                t_mm = s_t.n
                for ih in range(8):
                    xnT_busy[ih] = t_mm
                wq_busy[wsl] = t_mm
                if si + 4 < len(CHUNKS):
                    issue_wq(si + 4)
                dsth = slice(1024 * h, 1024 * (h + 1))
                if kind == "v":
                    t_v = b.op("vector", lambda e, fc=fc, dsth=dsth, PSX=PSX:
                               e.tensor_copy(v_fm[:, fc, dsth], PSX[:, 0:1024]),
                               wait=[(s_t, t_mm)], inc=s_v)
                    ps_free[psl] = t_v
                    b.op("gpsimd", lambda e, fc=fc, dsth=dsth: e.tensor_tensor(
                        kv_fm[:, fc, dsth], k_fm[:, fc, dsth], v_fm[:, fc, dsth],
                        ALU.mult), wait=[(s_v, t_v), (s_v, tkp[fc])], inc=s_p)
                else:
                    fsl = si % 2
                    t_r = b.op("scalar", lambda e, fsl=fsl, PSX=PSX: e.activation(
                        phi_f[:, fsl, :], PSX[:, 0:1024], AF.Relu, scale=-1.0),
                        wait=[(s_t, t_mm)], inc=s_a)
                    t_e = b.op("scalar", lambda e, fsl=fsl: e.activation(
                        phi_e[:, 2 + fsl, :], phi_f[:, fsl, :], AF.Exp, scale=-1.0),
                        wait=[(s_v, phi_busy[fsl])], inc=s_a)
                    dst = qp_fm if kind == "q" else k_fm
                    t_p = b.op("vector", lambda e, fc=fc, fsl=fsl, dst=dst, dsth=dsth,
                               PSX=PSX: e.scalar_tensor_tensor(
                                   dst[:, fc, dsth], PSX[:, 0:1024], 1.0,
                                   phi_e[:, 2 + fsl, :], ALU.add, ALU.max),
                               wait=[(s_a, t_e)], inc=s_v)
                    ps_free[psl] = t_p
                    phi_busy[fsl] = t_p
                    if kind == "k":
                        tkp[fc] = t_p

        b.barrier(s_ph, DMAS)

        iota_f = A8.bitcast(F32)
        sinrow_f = A9.bitcast(F32)
        agv = A9[0:8, 0:T]
        P1buf = A9.rearrange("p (r t) -> p r t", t=T)
        prodb = A8.rearrange("p (r t) -> p r t", t=T)
        denf = A9.bitcast(F32)

        t_iota = sdma(iota_f[:, :], iota_d[:, :])

        scan_tok = []
        for c in range(8):
            b.op("vector", lambda e, c=c: e.tensor_tensor_scan(
                kv_fm[:, c, :], dpar_sb[:, c:c + 1].broadcast_to([P, T]), kv_fm[:, c, :],
                0.0, ALU.mult, ALU.add), inc=s_v)
            scan_tok.append(b.op("vector", lambda e, c=c: e.tensor_tensor_scan(
                k_fm[:, c, :], dpar_sb[:, c:c + 1].broadcast_to([P, T]), k_fm[:, c, :],
                0.0, ALU.mult, ALU.add), inc=s_v))

        vop(lambda e: e.tensor_copy(
            sout_sb.rearrange("p (c o) -> p c o", o=1)[:, 0:8, :], kv_fm[:, :, T - 1:T]),
            wait=[(s_v, scan_tok[-1])])
        t_so = vop(lambda e: e.tensor_copy(
            sout_sb.rearrange("p (c o) -> p c o", o=1)[:, 8:16, :], k_fm[:, :, T - 1:T]),
            inc=s_v)
        t_sup = sdma(cc_s_in[:, :], sout_sb[:, :], wait=[(s_v, t_so)])
        t_sag = b.op("gpsimd", lambda e: e.collective_compute(
            "AllGather", ALU.bypass, replica_groups=[list(range(NCORE))],
            ins=[cc_s_in.ap().opt()], outs=[cc_s_out.ap().opt()]),
            wait=[(s_sy, t_sup)], inc=s_cc)
        t_agr = gdma(agv, cc_s_out.rearrange("(r p) c -> r (p c)", p=128),
                     wait=[(s_cc, t_sag)])
        t_selmm = None
        for nsl in range(4):
            t_selmm = b.op("tensor", lambda e, nsl=nsl: e.matmul(
                PS2[0:1, 512 * nsl:512 * (nsl + 1)], sel_sb[:, :],
                agv[:, 512 * nsl:512 * (nsl + 1)], start=True, stop=True),
                wait=[(s_gd, t_agr)] if nsl == 0 else [], inc=(s_t if nsl == 3 else None))
        t_sinrow = b.op("scalar", lambda e: e.copy(sinrow_f[0:1, :], PS2[0:1, :]),
                        wait=[(s_t, t_selmm)], inc=s_a)
        t_sio = sdma(sin_dram[:, :], sinrow_f[0:1, :], wait=[(s_a, t_sinrow)])
        t_sin = sdma(sinfm_sb[:, :], sin_dram.rearrange("o (p c) -> (o p) c", c=16),
                     wait=[(s_sy, t_sio)])

        t_corr = [0] * 8
        t_p1_last = 0
        for c in range(8):
            pb = c % 2
            w0 = [(s_sy, t_sin), (s_sy, t_iota)] if c == 0 else []
            if c >= 2:
                w0 = [(s_v, t_corr[c - 2])]
            t_p1 = b.op("scalar", lambda e, c=c, pb=pb: e.activation(
                P1buf[:, pb, :], iota_f[:, :], AF.Exp, scale=lnd_sb[:, c:c + 1]),
                wait=w0, inc=s_a)
            t_p1_last = t_p1
            b.op("vector", lambda e, c=c, pb=pb: e.scalar_tensor_tensor(
                kv_fm[:, c, :], P1buf[:, pb, :], sinfm_sb[:, c:c + 1], kv_fm[:, c, :],
                ALU.mult, ALU.add), wait=[(s_a, t_p1), (s_v, scan_tok[-1])], inc=s_v)
            t_corr[c] = b.op("vector", lambda e, c=c, pb=pb: e.scalar_tensor_tensor(
                k_fm[:, c, :], P1buf[:, pb, :], sinfm_sb[:, 8 + c:9 + c], k_fm[:, c, :],
                ALU.mult, ALU.add), inc=s_v)

        t_pr = [0] * 8
        t_dmm = None
        for c in range(8):
            pb = c % 2
            t_pr[c] = b.op("gpsimd", lambda e, c=c, pb=pb: e.tensor_tensor(
                prodb[:, pb, :], qp_fm[:, c, :], k_fm[:, c, :], ALU.mult),
                wait=[(s_v, t_corr[c]), (s_a, t_p1_last)] +
                     ([(s_t, t_dmm)] if c >= 2 else []), inc=s_p)
            for nsl in range(4):
                t_dmm = b.op("tensor", lambda e, c=c, pb=pb, nsl=nsl: e.matmul(
                    PS2[0:16, 512 * nsl:512 * (nsl + 1)], hs16_sb[:, c, :],
                    prodb[:, pb, 512 * nsl:512 * (nsl + 1)],
                    start=(c == 0), stop=(c == 7)),
                    wait=[(s_p, t_pr[c]), (s_a, t_sinrow)] if nsl == 0 else [],
                    inc=(s_t if nsl == 3 else None))
        t_dene = b.op("vector", lambda e: e.tensor_scalar_max(
            denf[0:16, :], PS2[0:16, :], 1e-6),
            wait=[(s_t, t_dmm), (s_v, t_corr[7])], inc=s_v)
        b.op("vector", lambda e: e.reciprocal(denf[0:16, :], denf[0:16, :]), inc=s_v)
        t_recip = b.op("vector", lambda e: e.tensor_copy(recipb_sb[:, :], denf[0:16, :]),
                       inc=s_v)

        rb_free = [0, 0, 0, 0]
        t_outc = [0] * 8
        last_num = 0
        for c in range(8):
            pb = c % 2
            t_num = b.op("gpsimd", lambda e, c=c, pb=pb: e.tensor_tensor(
                prodb[:, pb, :], qp_fm[:, c, :], kv_fm[:, c, :], ALU.mult),
                wait=[(s_v, t_recip)] + ([(s_v, t_outc[c - 2])] if c >= 2 else []),
                inc=s_p)
            for nsl in range(4):
                t_rb = b.op("tensor", lambda e, c=c, nsl=nsl: e.matmul(
                    PS1[:, 512 * nsl:512 * (nsl + 1)], hs16t_sb[:, c, :],
                    recipb_sb[:, 512 * nsl:512 * (nsl + 1)], start=True, stop=True),
                    wait=[(s_p, t_num), (s_v, rb_free[nsl])] if nsl == 0
                    else [(s_v, rb_free[nsl])],
                    inc=s_t)
                tok = b.op("vector", lambda e, c=c, pb=pb, nsl=nsl: e.tensor_tensor(
                    kv_fm[:, c, 512 * nsl:512 * (nsl + 1)],
                    prodb[:, pb, 512 * nsl:512 * (nsl + 1)],
                    PS1[:, 512 * nsl:512 * (nsl + 1)], ALU.mult),
                    wait=[(s_t, t_rb)], inc=s_v)
                rb_free[nsl] = tok
            t_outc[c] = tok

        dbg("out_fm", kv_fm[:, :, :])
        dbg("runk_fm", k_fm[:, :, :])
        dbg("recipb", recipb_sb[:, :])
        dbg("sout", sout_sb[:, :])
        dbg("sinfm", sinfm_sb[:, :])
        dbg("dpar", dpar_sb[:, :])
        dbg("mq", mq_sb[:, :])
        b.barrier(s_ph, DMAS)

        wg_sb = A6.rearrange("p (k n) -> p k n", n=1024)
        wo_sb = A7.rearrange("p (k n) -> p k n", n=1024)
        g_fm = k_fm
        mix_fm = kv_fm
        af_t = A8.rearrange("p (r d) -> p r d", d=D)
        dbuf = A9.rearrange("p (r t) -> p r t", t=T)

        t_wg = gdma(wg_sb[:, :, :], wgate_d.rearrange("(k p) n -> p k n", p=128))
        t_wo = gdma(wo_sb[:, :, :], wout_d.rearrange("(k p) n -> p k n", p=128))

        t_gev = [0] * 8
        for oc in range(8):
            ps = (PS1, PS2)[oc % 2]
            mmtok = None
            for nsl in range(4):
                for kc in range(8):
                    mmtok = b.op("tensor", lambda e, oc=oc, nsl=nsl, kc=kc, ps=ps:
                                 e.matmul(ps[:, 512 * nsl:512 * (nsl + 1)],
                                          wg_sb[:, kc, 128 * oc:128 * (oc + 1)],
                                          kv_fm[:, kc, 512 * nsl:512 * (nsl + 1)],
                                          start=(kc == 0), stop=(kc == 7)),
                                 wait=([(s_gd, t_wg), (s_a, t_gev[oc - 2] if oc >= 2 else 0)]
                                       if (nsl == 0 and kc == 0) else []),
                                 inc=(s_t if (nsl == 3 and kc == 7) else None))
            t_gev[oc] = b.op("scalar", lambda e, oc=oc, ps=ps: e.activation(
                g_fm[:, oc, :], ps[:, :], AF.Sigmoid, bias=bgfm_sb[:, oc:oc + 1]),
                wait=[(s_t, mmtok)], inc=s_a)
        gate_mm_all = mmtok
        t_mx = [0] * 8
        for oc in range(8):
            pb = oc % 2
            b.op("gpsimd", lambda e, oc=oc, pb=pb: e.tensor_tensor(
                dbuf[:, pb, :], kv_fm[:, oc, :], v_fm[:, oc, :], ALU.subtract),
                wait=[(s_a, t_gev[oc]), (s_t, gate_mm_all)] +
                     ([(s_p, t_mx[oc - 2])] if oc >= 2 else []),
                inc=s_p)
            b.op("gpsimd", lambda e, oc=oc, pb=pb: e.tensor_tensor(
                dbuf[:, pb, :], dbuf[:, pb, :], g_fm[:, oc, :], ALU.mult), inc=s_p)
            t_mx[oc] = b.op("gpsimd", lambda e, oc=oc, pb=pb: e.tensor_tensor(
                mix_fm[:, oc, :], dbuf[:, pb, :], v_fm[:, oc, :], ALU.add), inc=s_p)
        t_mix_all = t_mx[7]
        t_afs = [0] * NT
        for i in range(NT):
            ps = (PS1, PS2)[i % 2]
            mmtok = None
            for nsl in range(2):
                for kc in range(8):
                    mmtok = b.op("tensor", lambda e, i=i, nsl=nsl, kc=kc, ps=ps:
                                 e.matmul(ps[:, 512 * nsl:512 * (nsl + 1)],
                                          mix_fm[:, kc, 128 * i:128 * (i + 1)],
                                          wo_sb[:, kc, 512 * nsl:512 * (nsl + 1)],
                                          start=(kc == 0), stop=(kc == 7)),
                                 wait=([(s_p, t_mix_all), (s_gd, t_wo),
                                        (s_sy, t_afs[i - 2] if i >= 2 else 0)]
                                       if (nsl == 0 and kc == 0) else []),
                                 inc=(s_t if (nsl == 1 and kc == 7) else None))
            t_ev = b.op("scalar", lambda e, i=i, ps=ps: e.activation(
                af_t[:, i % 2, :], ps[:, 0:1024], AF.Copy), wait=[(s_t, mmtok)], inc=s_a)
            t_afs[i] = sdma(af_dram.rearrange("(i p) d -> p i d", p=128)[:, i, :],
                            af_t[:, i % 2, :], wait=[(s_a, t_ev)])

        dbg("mix_fm", mix_fm[:, :, :])
        dbg("g_fm", g_fm[:, :, :])
        b.barrier(s_ph, DMAS)

        g1b = A1[:, 0:1024]
        b1lb = A1[:, 1024:2048]
        dqx_fm = A1[:, 2048:6144].rearrange("p (k t) -> p k t", t=512)
        eslot = A1.rearrange("p (r d) -> p r d", d=D)
        g2b = A6[:, 0:4096]
        b2lb = A6[:, 4096:8192]
        w2lo = A4.rearrange("p (k n) -> p k n", n=1024)
        w2hi = A5.rearrange("p (k n) -> p k n", n=1024)
        h2tm = A3.rearrange("p (i f) -> p i f", f=FF)
        w1s = A7[:, 0:4096].rearrange("p (r k q) -> p r k q", r=4, k=8)
        dqx2_t = A7[:, 4096:8192]
        afr_t = A8[:, 3072:4096]
        a2f = A2.bitcast(F32)
        ln1b = A2.rearrange("p (r d) -> p r d", d=D)
        ln2b = A2[:, 2048:10240].rearrange("p (r f) -> p r f", f=FF)
        ybuf = a2f.rearrange("p (r d) -> p r d", d=D)
        xq1g = A2.bitcast(I8)[:, 28672:32768].rearrange("p (j d) -> p j d", d=D)
        xq2_t = A8.bitcast(I8)[:, 0:4096]
        h2f_t = A8.rearrange("p (r q) -> p r q", q=512)
        dqx2fm = A9.rearrange("p (k q) -> p k q", q=128)

        gdma(g1b[:, :], g1row_d[:, :].broadcast_to([P, D]))
        gdma(b1lb[:, :], b1lrow_d[:, :].broadcast_to([P, D]))
        gdma(g2b[:, :], g2row_d[:, :].broadcast_to([P, FF]))
        t_bct = gdma(b2lb[:, :], b2lrow_d[:, :].broadcast_to([P, FF]))
        cdma(w2lo[:, :, :], cc_w2_out[0:2048, :].rearrange("(k p) n -> p k n", p=128),
             wait=[(s_cc, t_ag2)])
        t_w2 = cdma(w2hi[:, :, :], cc_w2_out[2048:4096, :].rearrange("(k p) n -> p k n",
                                                                     p=128))

        st = {
            "e_free": [0, 0],
            "ln1f_free": [0, 0],
            "ln2f_free": [0, 0],
            "dqx2t_free": 0,
            "dqxT_busy": 0,
            "dqx_grp": 0,
            "h2_busy": 0,
            "w1s_busy": [0, 0, 0, 0],
            "ps1_free": [0, 0, 0, 0],
            "h2f_free": [0, 0],
            "d2lo_busy": 0,
            "d2hi_busy": 0,
            "ps2h_free": [0, 0],
            "ybuf_free": [0, 0],
            "afr_free": 0,
            "h2grp": 0,
            "d2t": {},
            "t_af": {},
            "t_xh": {},
        }

        def emit_E_pair(g, j0):
            JJ = (j0, j0 + 1)
            tk = {}
            for j in JJ:
                i = g * GRP + j
                sl = j % 2
                t_x = sdma(xsv[:, :],
                           x_in.rearrange("(i p) d -> p i d", p=128)[:, i, :],
                           wait=[xs_free[0]] if xs_free[0][0] else [])
                t_xn = b.op("vector", lambda e, i=i, sl=sl: e.tensor_scalar(
                    eslot[:, 6 + sl, :], xsv[:, :], stat[:, 2, i:i + 1],
                    stat[:, 3, i:i + 1], ALU.add, ALU.mult),
                    wait=[(s_sy, t_x), (s_sy, st["e_free"][sl])], inc=s_v)
                xs_free[0] = (s_v, t_xn)
                tk[("xn", j)] = t_xn
            for j in JJ:
                i = g * GRP + j
                sl = j % 2
                vop(lambda e, i=i, sl=sl: e.reduce_sum(
                    stat[:, 0, i:i + 1], eslot[:, 6 + sl, :], AX.X),
                    wait=[(s_v, tk[("xn", j)])])
            for j in JJ:
                i = g * GRP + j
                sl = j % 2
                tk[("sq", j)] = b.op("scalar", lambda e, i=i, sl=sl: e.activation(
                    ln1b[:, sl, :], eslot[:, 6 + sl, :], AF.Square,
                    accum_out=stat[:, 1, i:i + 1]),
                    wait=[(s_v, tk[("xn", j)]), (s_v, st["ln1f_free"][sl])], inc=s_a)
            for j in JJ:
                i = g * GRP + j
                tk[("neg", j)] = vop(lambda e, i=i: e.tensor_scalar_mul(
                    stat[:, 4, i:i + 1], stat[:, 0, i:i + 1], -INV_D),
                    wait=[(s_a, tk[("sq", j)])], inc=s_v)
            for j in JJ:
                i = g * GRP + j
                tk[("m2", j)] = aop(lambda e, i=i: e.activation(
                    stat[:, 6, i:i + 1], stat[:, 4, i:i + 1], AF.Square),
                    wait=[(s_v, tk[("neg", j)])], inc=s_a)
            for j in JJ:
                i = g * GRP + j
                vop(lambda e, i=i: e.tensor_scalar(
                    stat[:, 5, i:i + 1], stat[:, 1, i:i + 1], INV_D, float(LN_EPS),
                    ALU.mult, ALU.add))
            for j in JJ:
                i = g * GRP + j
                tk[("v1", j)] = vop(lambda e, i=i: e.tensor_tensor(
                    stat[:, 5, i:i + 1], stat[:, 5, i:i + 1], stat[:, 6, i:i + 1],
                    ALU.subtract), wait=[(s_a, tk[("m2", j)])], inc=s_v)
            for j in JJ:
                i = g * GRP + j
                tk[("s1", j)] = aop(lambda e, i=i: e.activation(
                    stat[:, 5, i:i + 1], stat[:, 5, i:i + 1], AF.Sqrt),
                    wait=[(s_v, tk[("v1", j)])], inc=s_a)
            for j in JJ:
                i = g * GRP + j
                tk[("rs", j)] = vop(lambda e, i=i: e.reciprocal(
                    stat[:, 5, i:i + 1], stat[:, 5, i:i + 1]),
                    wait=[(s_a, tk[("s1", j)])], inc=s_v)
            for j in JJ:
                i = g * GRP + j
                sl = j % 2
                b.op("vector", lambda e, i=i, sl=sl: e.tensor_scalar_mul(
                    ln1b[:, sl, :], g1b[:, :], stat[:, 5, i:i + 1]),
                    wait=[(s_v, tk[("rs", j)]), (s_gd, t_bct)], inc=s_v)
            for j in JJ:
                i = g * GRP + j
                sl = j % 2
                b.op("vector", lambda e, i=i, sl=sl: e.scalar_tensor_tensor(
                    ln1b[:, sl, :], eslot[:, 6 + sl, :], stat[:, 4, i:i + 1],
                    ln1b[:, sl, :], ALU.add, ALU.mult), inc=s_v)
            for j in JJ:
                sl = j % 2
                b.op("vector", lambda e, sl=sl: e.tensor_tensor(
                    ln1b[:, sl, :], ln1b[:, sl, :], b1lb[:, :], ALU.add), inc=s_v)
            for j in JJ:
                i = g * GRP + j
                sl = j % 2
                vop(lambda e, i=i, sl=sl: e.reduce_max(
                    stat[:, 6, i:i + 1], ln1b[:, sl, :], AX.X,
                    apply_absolute_value=True))
            for j in JJ:
                i = g * GRP + j
                vop(lambda e, i=i: e.tensor_scalar_max(
                    stat[:, 6, i:i + 1], stat[:, 6, i:i + 1], 1e-5))
            for j in JJ:
                i = g * GRP + j
                vop(lambda e, i=i: e.reciprocal(
                    stat[:, 7, i:i + 1], stat[:, 6, i:i + 1]))
            for j in JJ:
                i = g * GRP + j
                tk[("s127", j)] = vop(lambda e, i=i: e.tensor_scalar_mul(
                    stat[:, 7, i:i + 1], stat[:, 7, i:i + 1], 127.0), inc=s_v)
            for j in JJ:
                i = g * GRP + j
                sl = j % 2
                b.op("vector", lambda e, i=i, j=j, sl=sl: e.tensor_scalar_mul(
                    xq1g[:, j, :], ln1b[:, sl, :], stat[:, 7, i:i + 1]),
                    wait=[(s_v, tk[("s127", j)])], inc=s_v)
            for j in JJ:
                i = g * GRP + j
                tk[("rs1", j)] = vop(lambda e, i=i: e.tensor_scalar_mul(
                    stat[:, 6, i:i + 1], stat[:, 6, i:i + 1], 1.0 / 127.0), inc=s_v)
            for j in JJ:
                i = g * GRP + j
                sl = j % 2
                t_dq = b.op("vector", lambda e, i=i, j=j, sl=sl: e.tensor_scalar_mul(
                    eslot[:, 6 + sl, :], xq1g[:, j, :], stat[:, 6, i:i + 1]),
                    wait=[(s_v, tk[("rs1", j)])], inc=s_v)
                st["ln1f_free"][sl] = t_dq
                st["e_free"][sl] = strp(
                    dqx_fm[:, :, 128 * j:128 * (j + 1)], eslot[:, 6 + sl, :],
                    wait=[(s_v, t_dq), (s_t, st["dqxT_busy"])])
            if j0 + 2 == GRP:
                st["dqx_grp"] = s_sy.n

        def emit_F(g):
            toks = {}
            def issue_slab(ofc):
                wsl = ofc % 4
                toks[ofc] = cdma(w1s[:, wsl, :, :],
                                 cc_w1_out[128 * ofc:128 * (ofc + 1), :]
                                 .rearrange("p (k m) -> p k m", m=128),
                                 wait=[(s_cc, t_ag1), (s_t, st["w1s_busy"][wsl])])
            for ofc in range(4):
                issue_slab(ofc)
            last_mm = None
            for ofc in range(NC2):
                wsl = ofc % 4
                psl = ofc % 4
                ps = PS1[:, 512 * psl:512 * (psl + 1)]
                for kc in range(8):
                    last_mm = b.op("tensor", lambda e, kc=kc, wsl=wsl, ps=ps:
                                   e.matmul(ps, w1s[:, wsl, kc, :],
                                            dqx_fm[:, kc, :],
                                            start=(kc == 0), stop=(kc == 7)),
                                   wait=([(s_sc, toks[ofc]), (s_sy, st["dqx_grp"]),
                                          (s_a, st["ps1_free"][psl])] if kc == 0 else []),
                                   inc=(s_t if kc == 7 else None))
                st["w1s_busy"][wsl] = last_mm
                if ofc + 4 < NC2:
                    issue_slab(ofc + 4)
                hsl = ofc % 2
                t_gelu = b.op("scalar", lambda e, ofc=ofc, ps=ps, hsl=hsl: e.activation(
                    h2f_t[:, 4 + hsl, :], ps, AF.Gelu,
                    bias=b1fm_sb[:, ofc:ofc + 1], scale=mq_sb[:, 0:1]),
                    wait=[(s_t, last_mm), (s_sy, st["h2f_free"][hsl])], inc=s_a)
                st["ps1_free"][psl] = t_gelu
                st["h2f_free"][hsl] = strp(
                    h2tm[:, :, 128 * ofc:128 * (ofc + 1)],
                    h2f_t[:, 4 + hsl, :],
                    wait=[(s_a, t_gelu), (s_v, st["h2_busy"])])
            st["dqxT_busy"] = last_mm

        def emit_G_pair(i0):
            II = (i0, i0 + 1)
            t_h2g = st["h2grp"]
            tok = {}
            for i in II:
                vop(lambda e, i=i: e.reduce_sum(
                    st2[:, 0, i:i + 1], h2tm[:, i % GRP, :], AX.X),
                    wait=[(s_sy, t_h2g)])
            for i in II:
                sl = i % 2
                tok[("sq", i)] = b.op("scalar", lambda e, i=i, sl=sl: e.activation(
                    ln2b[:, sl, :], h2tm[:, i % GRP, :], AF.Square,
                    accum_out=st2[:, 1, i:i + 1]),
                    wait=[(s_sy, t_h2g), (s_v, st["ln2f_free"][sl])], inc=s_a)
            for i in II:
                tok[("neg", i)] = vop(lambda e, i=i: e.tensor_scalar_mul(
                    st2[:, 2, i:i + 1], st2[:, 0, i:i + 1], -INV_FF),
                    wait=[(s_a, tok[("sq", i)])], inc=s_v)
            for i in II:
                tok[("m2", i)] = aop(lambda e, i=i: e.activation(
                    st2[:, 4, i:i + 1], st2[:, 2, i:i + 1], AF.Square),
                    wait=[(s_v, tok[("neg", i)])], inc=s_a)
            for i in II:
                vop(lambda e, i=i: e.tensor_scalar(
                    st2[:, 3, i:i + 1], st2[:, 1, i:i + 1], INV_FF, float(LN_EPS),
                    ALU.mult, ALU.add))
            for i in II:
                tok[("var", i)] = vop(lambda e, i=i: e.tensor_tensor(
                    st2[:, 3, i:i + 1], st2[:, 3, i:i + 1], st2[:, 4, i:i + 1],
                    ALU.subtract), wait=[(s_a, tok[("m2", i)])], inc=s_v)
            for i in II:
                tok[("sd", i)] = aop(lambda e, i=i: e.activation(
                    st2[:, 3, i:i + 1], st2[:, 3, i:i + 1], AF.Sqrt),
                    wait=[(s_v, tok[("var", i)])], inc=s_a)
            for i in II:
                tok[("rsd", i)] = vop(lambda e, i=i: e.reciprocal(
                    st2[:, 3, i:i + 1], st2[:, 3, i:i + 1]),
                    wait=[(s_a, tok[("sd", i)])], inc=s_v)
            for i in II:
                sl = i % 2
                b.op("vector", lambda e, i=i, sl=sl: e.tensor_scalar_mul(
                    ln2b[:, sl, :], g2b[:, :], st2[:, 3, i:i + 1]),
                    wait=[(s_v, tok[("rsd", i)]), (s_gd, t_bct)], inc=s_v)
            t_ln2 = None
            for i in II:
                sl = i % 2
                t_ln2 = b.op("vector", lambda e, i=i, sl=sl: e.scalar_tensor_tensor(
                    ln2b[:, sl, :], h2tm[:, i % GRP, :], st2[:, 2, i:i + 1],
                    ln2b[:, sl, :], ALU.add, ALU.mult), inc=s_v)
            st["h2_busy"] = t_ln2
            for i in II:
                sl = i % 2
                tok[("l2", i)] = b.op("vector", lambda e, sl=sl: e.tensor_tensor(
                    ln2b[:, sl, :], ln2b[:, sl, :], b2lb[:, :], ALU.add), inc=s_v)
            for i in II:
                sl = i % 2
                vop(lambda e, i=i, sl=sl: e.reduce_max(
                    st2[:, 4, i:i + 1], ln2b[:, sl, :], AX.X,
                    apply_absolute_value=True))
            for i in II:
                vop(lambda e, i=i: e.tensor_scalar_max(
                    st2[:, 4, i:i + 1], st2[:, 4, i:i + 1], 1e-5))
            for i in II:
                vop(lambda e, i=i: e.reciprocal(
                    st2[:, 5, i:i + 1], st2[:, 4, i:i + 1]))
            for i in II:
                tok[("s2", i)] = vop(lambda e, i=i: e.tensor_scalar_mul(
                    st2[:, 5, i:i + 1], st2[:, 5, i:i + 1], 127.0), inc=s_v)
            for i in II:
                vop(lambda e, i=i: e.tensor_scalar_mul(
                    st2[:, 6, i:i + 1], st2[:, 4, i:i + 1], 1.0 / 127.0))
            for i in II:
                tok[("rs2", i)] = vop(lambda e, i=i: e.tensor_tensor(
                    st2[:, 6, i:i + 1], st2[:, 6, i:i + 1], mq_sb[:, 1:2], ALU.mult),
                    inc=s_v)
            return tok

        def emit_G_tail(i, tok):
            sl = i % 2
            b.op("vector", lambda e, i=i, sl=sl: e.tensor_scalar_mul(
                xq2_t[:, :], ln2b[:, sl, :], st2[:, 5, i:i + 1]),
                wait=[(s_v, tok[("s2", i)])], inc=s_v)
            t_dq2 = b.op("vector", lambda e, i=i: e.tensor_scalar_mul(
                dqx2_t[:, :], xq2_t[:, :], st2[:, 6, i:i + 1]),
                wait=[(s_v, tok[("rs2", i)]), (s_sy, st["dqx2t_free"])], inc=s_v)
            st["ln2f_free"][sl] = t_dq2
            t_lo = strp(dqx2fm[:, 0:16, :], dqx2_t[:, 0:2048],
                        wait=[(s_v, t_dq2), (s_t, st["d2lo_busy"])])
            t_hi = strp(dqx2fm[:, 16:32, :], dqx2_t[:, 2048:4096],
                        wait=[(s_t, st["d2hi_busy"])])
            st["dqx2t_free"] = t_hi
            st["d2t"][i] = (t_lo, t_hi)
            st["t_af"][i] = gdma(afr_t[:, :],
                                 af_dram.rearrange("(i p) d -> p i d", p=128)[:, i, :],
                                 wait=[(s_p, st["afr_free"])])
            st["t_xh"][i] = sdma(xsv[:, :],
                                 x_in.rearrange("(i p) d -> p i d", p=128)[:, i, :],
                                 wait=[xs_free[0]] if xs_free[0][0] else [])

        def emit_H(i):
            psh = i % 2
            t_af = st["t_af"][i]
            t_xh = st["t_xh"][i]
            mmh = None
            t_lo, t_hi = st["d2t"][i]
            for kc in range(32):
                w2v = w2lo if kc < 16 else w2hi
                kcl = kc % 16
                for nsl in range(2):
                    wl = []
                    if kc == 0 and nsl == 0:
                        wl = [(s_sy, t_lo), (s_sc, t_w2),
                              (s_a, st["ps2h_free"][psh])]
                    elif kc == 16 and nsl == 0:
                        wl = [(s_sy, t_hi)]
                    mmh = b.op("tensor",
                               lambda e, kc=kc, kcl=kcl, nsl=nsl, psh=psh, w2v=w2v:
                               e.matmul(PS2[:, 1024 * psh + 512 * nsl:
                                            1024 * psh + 512 * (nsl + 1)],
                                        dqx2fm[:, kc, :],
                                        w2v[:, kcl, 512 * nsl:512 * (nsl + 1)],
                                        start=(kc == 0), stop=False),
                               wait=wl,
                               inc=(s_t if (kc == 15 and nsl == 1) else None))
                    if kc == 15 and nsl == 1:
                        st["d2lo_busy"] = mmh
            for nsl in range(2):
                mmh = b.op("tensor", lambda e, nsl=nsl, psh=psh: e.matmul(
                    PS2[:, 1024 * psh + 512 * nsl:1024 * psh + 512 * (nsl + 1)],
                    ones1[0:1, 0:128], b2row_sb[0:1, 512 * nsl:512 * (nsl + 1)],
                    start=False, stop=True),
                    inc=(s_t if nsl == 1 else None))
            st["d2hi_busy"] = mmh
            t_ycp = b.op("scalar", lambda e, psh=psh: e.activation(
                ybuf[:, 5 + psh, :], PS2[:, 1024 * psh:1024 * psh + 1024], AF.Copy),
                wait=[(s_t, mmh), (s_gd, st["ybuf_free"][psh])], inc=s_a)
            st["ps2h_free"][psh] = t_ycp
            t_ya = b.op("gpsimd", lambda e, psh=psh: e.tensor_tensor(
                ybuf[:, 5 + psh, :], ybuf[:, 5 + psh, :], xsv[:, :], ALU.add),
                wait=[(s_a, t_ycp), (s_sy, t_xh)], inc=s_p)
            xs_free[0] = (s_p, t_ya)
            t_yb = b.op("gpsimd", lambda e, psh=psh: e.tensor_tensor(
                ybuf[:, 5 + psh, :], ybuf[:, 5 + psh, :], afr_t[:, :], ALU.add),
                wait=[(s_gd, t_af)], inc=s_p)
            st["afr_free"] = t_yb
            st["ybuf_free"][psh] = gdma(
                y_out.rearrange("(i p) d -> p i d", p=128)[:, i, :],
                ybuf[:, 5 + psh, :], wait=[(s_p, t_yb)])

        for g in range(NGRP + 1):
            if g < NGRP:
                emit_E_pair(g, 0)
                emit_E_pair(g, 2)
            if g >= 1:
                i0 = GRP * (g - 1)
                gt0 = emit_G_pair(i0)
                emit_G_tail(i0, gt0)
                emit_H(i0)
                emit_G_tail(i0 + 1, gt0)
                gt1 = emit_G_pair(i0 + 2)
                emit_H(i0 + 1)
                emit_G_tail(i0 + 2, gt1)
                emit_H(i0 + 2)
                emit_G_tail(i0 + 3, gt1)
                emit_H(i0 + 3)
            if g < NGRP:
                emit_F(g)
                st["h2grp"] = s_sy.n


        b.op("sync", lambda e: e.nop(), wait=[(s_sy, s_sy.n)])
        b.op("scalar", lambda e: e.nop(), wait=[(s_sc, s_sc.n)])
        b.op("gpsimd", lambda e: e.nop(), wait=[(s_gd, s_gd.n)])
        b.flush()
    return nc


_NC_CACHE = {}


def _host_prepare(inputs):
    f32 = lambda a: np.ascontiguousarray(np.asarray(a), dtype=np.float32)
    x = f32(inputs["L0_attn_x"])
    Wqkv = f32(inputs["L0_attn_Wqkv"])
    Wout = f32(inputs["L0_attn_Wout"])
    Wgate = f32(inputs["L0_attn_Wgate"])
    bgate = f32(inputs["L0_attn_bgate"])
    dp = f32(inputs["L0_attn_decay_param"])
    g1 = f32(inputs["L0_ffn_ln1_g"])
    b1l = f32(inputs["L0_ffn_ln1_b"])
    w1 = f32(inputs["L0_ffn_w1"])
    b1 = f32(inputs["L0_ffn_b1"])
    g2 = f32(inputs["L0_ffn_ln2_g"])
    b2l = f32(inputs["L0_ffn_ln2_b"])
    w2 = f32(inputs["L0_ffn_w2"])
    b2 = f32(inputs["L0_ffn_b2"])

    order = list(range(8)) + [x for pair in ((8 + c, 16 + c) for c in range(8))
                              for x in pair]
    wq = Wqkv.reshape(24, 128, 1024)
    slabs = []
    for oc in order:
        blk = wq[oc]
        sl = blk.reshape(128, 8, 128).transpose(2, 1, 0)
        slabs.append(sl.reshape(128, 1024))
    wqkv_cols = np.ascontiguousarray(np.concatenate(slabs, axis=0))
    wgateT = np.ascontiguousarray(Wgate.T)
    woutT = np.ascontiguousarray(Wout.T)
    w2T = np.ascontiguousarray(w2.T)

    bgate_fm = np.ascontiguousarray(bgate.reshape(8, 128).T)
    b1_fm = np.ascontiguousarray(b1.reshape(32, 128).T)
    idx = (np.arange(128)[:, None] // 64) + 2 * np.arange(8)[None, :]
    param_fm = np.ascontiguousarray(dp[idx])
    iota_bc = np.ascontiguousarray(
        np.broadcast_to(np.arange(1, T + 1, dtype=np.float32), (P, T)))
    fc = np.arange(8)[None, :, None]
    p_ = np.arange(128)[:, None, None]
    m_ = np.arange(16)[None, None, :]
    hs16 = (2 * fc + p_ // 64 == m_).astype(np.float32).reshape(128, 128)
    m2_ = np.arange(16)[:, None, None]
    fc2 = np.arange(8)[None, :, None]
    p2_ = np.arange(128)[None, None, :]
    hs16t = (2 * fc2 + p2_ // 64 == m2_).astype(np.float32).reshape(16, 1024)

    shared = {
        "wqkv_cols": wqkv_cols, "wgateT": wgateT, "woutT": woutT,
        "bgate_fm": bgate_fm, "b1_fm": b1_fm,
        "b2_row": b2.reshape(1, D), "g1_row": g1.reshape(1, D),
        "b1l_row": b1l.reshape(1, D), "g2_row": g2.reshape(1, FF),
        "b2l_row": b2l.reshape(1, FF), "param_fm": param_fm,
        "iota_bc": iota_bc, "hs16_in": hs16, "hs16t_in": hs16t,
    }
    in_maps = []
    for c in range(NCORE):
        bseq, half = c // 2, c % 2
        sel = np.zeros((8, 1), dtype=np.float32)
        if half == 1:
            sel[c - 1, 0] = 1.0
        m = dict(shared)
        m["x_in"] = np.ascontiguousarray(x[bseq, half * T:(half + 1) * T])
        m["w1_shard"] = np.ascontiguousarray(w1[512 * c:512 * (c + 1), :])
        m["w2t_shard"] = np.ascontiguousarray(w2T[512 * c:512 * (c + 1), :])
        m["sel8"] = sel
        in_maps.append(m)
    return in_maps


def kernel(**inputs):
    if "nc" not in _NC_CACHE:
        _NC_CACHE["nc"] = build_nc()
    nc = _NC_CACHE["nc"]
    in_maps = _host_prepare(inputs)
    res = run_bass_kernel_spmd(nc, in_maps, core_ids=list(range(NCORE)))
    out = np.zeros((4, 4096, 1024), dtype=np.float32)
    for c in range(NCORE):
        bseq, half = c // 2, c % 2
        out[bseq, half * T:(half + 1) * T] = np.asarray(res.results[c]["y_out"])
    return out


# version 12
# speedup vs baseline: 1.0096x; 1.0096x over previous
import numpy as np
import ml_dtypes
import concourse.bass as bass
import concourse.mybir as mybir
from concourse.bass_utils import run_bass_kernel_spmd

F32 = mybir.dt.float32
BF16 = mybir.dt.bfloat16
I8 = mybir.dt.int8
AF = mybir.ActivationFunctionType
ALU = mybir.AluOpType
AX = mybir.AxisListType

P = 128
NT = 16
T = 2048
D = 1024
DQ = 3072
FF = 4096
NCORE = 8
GRP = 4
NGRP = NT // GRP
NC1 = 8
NC2 = 32
INV_D = 1.0 / D
INV_FF = 1.0 / FF
INV_NW = 1.0 / (4096.0 * 1024.0)
LN_EPS = 1e-5
ENGINES = ("sync", "tensor", "vector", "scalar", "gpsimd")


class Sem:
    __slots__ = ("h", "n")

    def __init__(self, h):
        self.h = h
        self.n = 0


class Bld:
    def __init__(self, nc):
        self.nc = nc
        self.q = {e: [] for e in ENGINES}

    def op(self, eng, fn, wait=(), inc=None, inc_k=1):
        waits = [(s.h, v) for (s, v) in wait if v]
        tgt = None
        if inc is not None:
            inc.n += inc_k
            tgt = (inc.h, inc_k)

        def f(e):
            for h, v in waits:
                e.wait_ge(h, v)
            ins = fn(e)
            if tgt is not None:
                ins.then_inc(*tgt)
        self.q[eng].append(f)
        return inc.n if inc is not None else None

    def flush(self):
        with self.nc.Block() as block:
            for e in ENGINES:
                ops = self.q[e]
                if not ops:
                    continue

                def make_body(ops):
                    def body(engine):
                        for f in ops:
                            f(engine)
                    return body
                getattr(block, e)(make_body(ops))
        self.q = {e: [] for e in ENGINES}

    def barrier(self, s_ph, dmas):
        for e in ENGINES:
            waits = [(dmas[e].h, dmas[e].n)] if e in dmas else []

            def f(engine, waits=tuple(waits), h=s_ph.h):
                for hh, v in waits:
                    engine.wait_ge(hh, v)
                engine.drain()
                engine.nop().then_inc(h, 1)
            self.q[e].append(f)
        s_ph.n += 5
        val = s_ph.n
        for e in ENGINES:
            def g(engine, h=s_ph.h, v=val):
                engine.wait_ge(h, v)
            self.q[e].append(g)


def build_nc(debug=False):
    nc = bass.Bass("TRN2", target_bir_lowering=False)
    dbg_reg = {}

    x_in = nc.dram_tensor("x_in", [T, D], F32, kind="ExternalInput")
    wqkv_d = nc.dram_tensor("wqkv_cols", [24 * 128, 1024], F32, kind="ExternalInput")
    wgate_d = nc.dram_tensor("wgateT", [D, D], F32, kind="ExternalInput")
    wout_d = nc.dram_tensor("woutT", [D, D], F32, kind="ExternalInput")
    w1sh_d = nc.dram_tensor("w1_shard", [512, D], F32, kind="ExternalInput")
    w2sh_d = nc.dram_tensor("w2t_shard", [512, D], F32, kind="ExternalInput")
    bgfm_d = nc.dram_tensor("bgate_fm", [P, 8], F32, kind="ExternalInput")
    b1fm_d = nc.dram_tensor("b1_fm", [P, 32], F32, kind="ExternalInput")
    b2row_d = nc.dram_tensor("b2_row", [1, D], F32, kind="ExternalInput")
    g1row_d = nc.dram_tensor("g1_row", [1, D], F32, kind="ExternalInput")
    b1lrow_d = nc.dram_tensor("b1l_row", [1, D], F32, kind="ExternalInput")
    g2row_d = nc.dram_tensor("g2_row", [1, FF], F32, kind="ExternalInput")
    b2lrow_d = nc.dram_tensor("b2l_row", [1, FF], F32, kind="ExternalInput")
    parfm_d = nc.dram_tensor("param_fm", [P, 8], F32, kind="ExternalInput")
    sel8_d = nc.dram_tensor("sel8", [8, 1], F32, kind="ExternalInput")
    iota_d = nc.dram_tensor("iota_bc", [P, T], F32, kind="ExternalInput")
    hs16_d = nc.dram_tensor("hs16_in", [P, 128], F32, kind="ExternalInput")
    hs16t_d = nc.dram_tensor("hs16t_in", [16, 1024], F32, kind="ExternalInput")
    y_out = nc.dram_tensor("y_out", [T, D], F32, kind="ExternalOutput")

    cc_m_in = nc.dram_tensor("cc_m_in", [1, 2], F32)
    cc_m_out = nc.dram_tensor("cc_m_out", [1, 2], F32)
    cc_w1_in = nc.dram_tensor("cc_w1_in", [512, 1024], BF16)
    cc_w1_out = nc.dram_tensor("cc_w1_out", [4096, 1024], BF16)
    cc_w2_in = nc.dram_tensor("cc_w2_in", [512, 1024], BF16)
    cc_w2_out = nc.dram_tensor("cc_w2_out", [4096, 1024], BF16)
    cc_s_in = nc.dram_tensor("cc_s_in", [P, 16], F32)
    cc_s_out = nc.dram_tensor("cc_s_out", [1024, 16], F32)
    sin_dram = nc.dram_tensor("sin_dram", [1, T], F32)
    af_dram = nc.dram_tensor("af_dram", [T, D], BF16)

    import contextlib
    stack = contextlib.ExitStack()
    with stack:
        mksem = lambda n: Sem(stack.enter_context(nc.semaphore(n)))
        s_sy = mksem("s_sy")
        s_sc = mksem("s_sc")
        s_gd = mksem("s_gd")
        s_cc = mksem("s_cc")
        s_t = mksem("s_t")
        s_v = mksem("s_v")
        s_a = mksem("s_a")
        s_p = mksem("s_p")
        s_ph = mksem("s_ph")
        DMAS = {"sync": s_sy, "scalar": s_sc, "gpsimd": s_gd}

        sbt = lambda n, s, dt: stack.enter_context(nc.sbuf_tensor(n, s, dt))

        A1 = sbt("A1", [P, 8192], BF16)
        A2 = sbt("A2", [P, 16384], BF16)
        A3 = sbt("A3", [P, 16384], BF16)
        A4 = sbt("A4", [P, 16384], BF16)
        A5 = sbt("A5", [P, 16384], BF16)
        A6 = sbt("A6", [P, 8192], BF16)
        A7 = sbt("A7", [P, 8192], BF16)
        A8 = sbt("A8", [P, 4096], BF16)
        A9 = sbt("A9", [P, 4096], BF16)
        XS = sbt("XS", [P, 1024], F32)
        gapv = sbt("gapv", [P, 256], BF16)
        gapa = sbt("gapa", [P, 256], BF16)

        SM = sbt("SM", [P, 3328], BF16)
        SMf = SM.bitcast(F32)
        mq_sb = SMf[:, 0:2]
        sw_sb = SMf[:, 2:4]
        wcol = SMf[:, 4:6]
        bgfm_sb = SMf[:, 8:16]
        dpar_sb = SMf[:, 16:24]
        lnd_sb = SMf[:, 24:32]
        b1fm_sb = SMf[:, 32:64]
        stat = SMf[:, 64:192].rearrange("p (a b) -> p a b", b=NT)
        st2 = SMf[:, 192:320].rearrange("p (a b) -> p a b", b=NT)
        sinfm_sb = SMf[:, 320:336]
        sout_sb = SMf[:, 336:352]
        wtot = SMf[0:1, 352:354]
        ones1 = SM[0:1, 768:896]
        sel_sb = SM[0:8, 896:897]
        hs16_sb = SM[:, 1024:1152].rearrange("p (a b) -> p a b", b=NT)
        hs16t_sb = SM[0:16, 1152:2176].rearrange("p (a b) -> p a b", b=128)
        b2row_sb = SM[0:1, 2176:3200]
        recipb_sb = sbt("recipb_sb", [16, T], BF16)

        xq1g = A2.bitcast(I8)[:, 28672:32768].rearrange("p (j d) -> p j d", d=D)
        PS1 = stack.enter_context(nc.psum_tensor("PS1", [P, 2048], F32))
        PS2 = stack.enter_context(nc.psum_tensor("PS2", [P, 2048], F32))

        b = Bld(nc)

        def gdma(dst, src, wait=()):
            return b.op("gpsimd", lambda e, d=dst, s=src: e.dma_start(d, s), wait=wait,
                        inc=s_gd, inc_k=16)

        def sdma(dst, src, wait=()):
            return b.op("sync", lambda e, d=dst, s=src: e.dma_start(d, s), wait=wait,
                        inc=s_sy, inc_k=16)

        def cdma(dst, src, wait=()):
            return b.op("scalar", lambda e, d=dst, s=src: e.dma_start(d, s), wait=wait,
                        inc=s_sc, inc_k=16)

        def strp(dst, src, wait=()):
            return b.op("sync", lambda e, d=dst, s=src: e.dma_start_transpose(d, s),
                        wait=wait, inc=s_sy, inc_k=16)

        def vop(fn, wait=(), inc=None):
            b.op("vector", fn, wait=wait)
            return b.op("vector", lambda e: e.tensor_copy(gapv[:, :], gapv[:, :]), inc=inc)

        def aop(fn, wait=(), inc=None):
            b.op("scalar", fn, wait=wait)
            return b.op("scalar", lambda e: e.copy(gapa[:, :], gapa[:, :]), inc=inc)

        def dbg(name, ap, wait=()):
            if not debug:
                return
            shape = list(ap.shape)
            dt = ap.dtype
            d = nc.dram_tensor("dbg_" + name, shape, dt, kind="ExternalOutput")
            idx = [slice(None)] * len(shape)
            b.op("sync", lambda e, d=d, s=ap: e.dma_start(d[tuple(idx)], s),
                 wait=wait, inc=s_sy, inc_k=16)

        wsh1 = A4.bitcast(F32)[:, 0:4096].rearrange("p (c n) -> p c n", n=D)
        wsh2 = A5.bitcast(F32)[:, 0:4096].rearrange("p (c n) -> p c n", n=D)
        tthr = A2.bitcast(F32)
        ti8 = A3.bitcast(I8)
        tern1 = A6[:, 0:4096].rearrange("p (c n) -> p c n", n=D)
        tern2 = A7[:, 0:4096].rearrange("p (c n) -> p c n", n=D)
        ttr = A8[:, 0:2048].rearrange("p (a k q) -> p a k q", a=2, k=8)
        wabs = A9

        gdma(hs16_sb[:, :, :], hs16_d.rearrange("p (c m) -> p c m", m=16))
        gdma(hs16t_sb[:, :, :], hs16t_d.rearrange("m (c p) -> m c p", p=128))
        gdma(sel_sb[:, :], sel8_d[:, :])
        gdma(b2row_sb[:, :], b2row_d[:, :])
        b.op("gpsimd", lambda e: e.memset(ones1[:, :], 1.0), inc=s_gd, inc_k=1)
        sdma(bgfm_sb[:, :], bgfm_d[:, :])
        sdma(b1fm_sb[:, :], b1fm_d[:, :])
        t_par = sdma(dpar_sb[:, :], parfm_d[:, :])
        sdma(wsh1[:, :, :], w1sh_d.rearrange("(c p) n -> p c n", p=128))
        t_wsh = sdma(wsh2[:, :, :], w2sh_d.rearrange("(c p) n -> p c n", p=128))

        t_sig = aop(lambda e: e.activation(dpar_sb[:, :], dpar_sb[:, :], AF.Sigmoid),
                    wait=[(s_sy, t_par)], inc=s_a)
        vop(lambda e: e.tensor_scalar_max(dpar_sb[:, :], dpar_sb[:, :], 1e-8),
            wait=[(s_a, t_sig)])
        t_dcp = vop(lambda e: e.tensor_copy(lnd_sb[:, :], dpar_sb[:, :]), inc=s_v)
        t_lnd = aop(lambda e: e.activation(lnd_sb[:, :], lnd_sb[:, :], AF.Ln),
                    wait=[(s_v, t_dcp)], inc=s_a)

        b.op("vector", lambda e: e.reduce_sum(
            wcol[:, 0:1], wsh1.rearrange("p c n -> p (c n)")[:, :], AX.X,
            apply_absolute_value=True), wait=[(s_sy, t_wsh)], inc=s_v)
        t_abs = b.op("vector", lambda e: e.reduce_sum(
            wcol[:, 1:2], wsh2.rearrange("p c n -> p (c n)")[:, :], AX.X,
            apply_absolute_value=True), inc=s_v)
        t_wt = b.op("gpsimd", lambda e: e.reduce_sum(wtot[:, :], wcol[:, :], AX.C),
                    wait=[(s_v, t_abs)], inc=s_p)
        t_mup = sdma(cc_m_in[:, :], wtot[:, :], wait=[(s_p, t_wt)])
        t_mar = b.op("gpsimd", lambda e: e.collective_compute(
            "AllReduce", ALU.add, replica_groups=[list(range(NCORE))],
            ins=[cc_m_in.ap().opt()], outs=[cc_m_out.ap().opt()]),
            wait=[(s_sy, t_mup)], inc=s_cc)
        t_mbc = sdma(mq_sb[:, :], cc_m_out[:, :].broadcast_to([P, 2]), wait=[(s_cc, t_mar)])
        vop(lambda e: e.tensor_scalar(mq_sb[:, :], mq_sb[:, :], INV_NW, 1e-5,
                                      ALU.mult, ALU.max), wait=[(s_sy, t_mbc)])
        vop(lambda e: e.reciprocal(sw_sb[:, :], mq_sb[:, :]), inc=s_v)
        tern_tok = []
        for j in range(2):
            wsh = (wsh1, wsh2)[j]
            tern = (tern1, tern2)[j]
            b.op("vector", lambda e, wsh=wsh, j=j: e.tensor_scalar_mul(
                tthr[:, :4096], wsh.rearrange("p c n -> p (c n)")[:, :],
                sw_sb[:, j:j + 1]), inc=s_v)
            b.op("vector", lambda e: e.tensor_scalar_mul(ti8[:, :4096], tthr[:, :4096],
                                                         1.0), inc=s_v)
            b.op("vector", lambda e: e.tensor_scalar_max(ti8[:, :4096], ti8[:, :4096],
                                                         -1.0), inc=s_v)
            b.op("vector", lambda e: e.tensor_scalar_min(ti8[:, :4096], ti8[:, :4096],
                                                         1.0), inc=s_v)
            tern_tok.append(b.op("vector", lambda e, tern=tern: e.tensor_copy(
                tern.rearrange("p c n -> p (c n)")[:, :], ti8[:, :4096]), inc=s_v))

        for o4 in range(4):
            tt = strp(ttr[:, o4 % 2, :, :], tern1[:, o4, :], wait=[(s_v, tern_tok[0])])
            sdma(cc_w1_in[o4 * 128:(o4 + 1) * 128, :].rearrange("p (k q) -> p k q", q=128),
                 ttr[:, o4 % 2, :, :], wait=[(s_sy, tt)])
        sdma(cc_w2_in.rearrange("(c p) n -> p c n", p=128), tern2[:, :, :],
             wait=[(s_v, tern_tok[1])])
        t_wcon = s_sy.n
        t_ag1 = b.op("gpsimd", lambda e: e.collective_compute(
            "AllGather", ALU.bypass, replica_groups=[list(range(NCORE))],
            ins=[cc_w1_in.ap().opt()], outs=[cc_w1_out.ap().opt()]),
            wait=[(s_sy, t_wcon)], inc=s_cc)
        t_ag2 = b.op("gpsimd", lambda e: e.collective_compute(
            "AllGather", ALU.bypass, replica_groups=[list(range(NCORE))],
            ins=[cc_w2_in.ap().opt()], outs=[cc_w2_out.ap().opt()]), inc=s_cc)

        b.barrier(s_ph, DMAS)

        xnT = A1.rearrange("p (k q) -> p k q", q=1024)
        qp_fm = A2.rearrange("p (c t) -> p c t", t=T)
        v_fm = A3.rearrange("p (c t) -> p c t", t=T)
        kv_fm = A4.rearrange("p (c t) -> p c t", t=T)
        k_fm = A5.rearrange("p (c t) -> p c t", t=T)
        wq_s = A6[:, 0:4096].rearrange("p (r k q) -> p r k q", r=4, k=8)
        xsv = XS
        xn_t = A8.rearrange("p (r d) -> p r d", d=D)
        a9f = A9.bitcast(F32)
        phi_f = a9f.rearrange("p (r q) -> p r q", q=1024)
        phi_e = A8.rearrange("p (r q) -> p r q", q=1024)

        CHUNKS = [("q", c) for c in range(8)]
        for c in range(8):
            CHUNKS.append(("k", c))
            CHUNKS.append(("v", c))
        ps_free = [0, 0]
        wq_busy = [0, 0, 0, 0]
        xnT_busy = [0] * 8
        xs_free = [(None, 0)]
        xn_free = [0, 0]
        phi_busy = [0, 0]
        tkp = [0] * 8

        for h in range(2):
            for ih in range(8):
                i = h * 8 + ih
                sl = ih % 2
                t_x = sdma(xsv[:, :], x_in.rearrange("(i p) d -> p i d", p=128)[:, i, :],
                           wait=[xs_free[0]] if xs_free[0][0] else [])
                vop(lambda e, i=i: e.reduce_sum(
                    stat[:, 0, i:i + 1], xsv[:, :], AX.X), wait=[(s_sy, t_x)])
                t_sq = b.op("scalar", lambda e, i=i: e.activation(
                    PS2[:, 1024:2048], xsv[:, :], AF.Square,
                    accum_out=stat[:, 1, i:i + 1]),
                    wait=[(s_sy, t_x)], inc=s_a)
                t_neg = vop(lambda e, i=i: e.tensor_scalar_mul(
                    stat[:, 2, i:i + 1], stat[:, 0, i:i + 1], -INV_D),
                    wait=[(s_a, t_sq)], inc=s_v)
                t_mu2 = aop(lambda e, i=i: e.activation(
                    stat[:, 4, i:i + 1], stat[:, 2, i:i + 1], AF.Square),
                    wait=[(s_v, t_neg)], inc=s_a)
                vop(lambda e, i=i: e.tensor_scalar(
                    stat[:, 3, i:i + 1], stat[:, 1, i:i + 1], INV_D, float(LN_EPS),
                    ALU.mult, ALU.add))
                t_var = vop(lambda e, i=i: e.tensor_tensor(
                    stat[:, 3, i:i + 1], stat[:, 3, i:i + 1], stat[:, 4, i:i + 1],
                    ALU.subtract), wait=[(s_a, t_mu2)], inc=s_v)
                t_std = aop(lambda e, i=i: e.activation(
                    stat[:, 3, i:i + 1], stat[:, 3, i:i + 1], AF.Sqrt),
                    wait=[(s_v, t_var)], inc=s_a)
                t_rst = vop(lambda e, i=i: e.reciprocal(
                    stat[:, 3, i:i + 1], stat[:, 3, i:i + 1]), wait=[(s_a, t_std)],
                    inc=s_v)
                t_xn = b.op("vector", lambda e, i=i, sl=sl: e.tensor_scalar(
                    xn_t[:, sl, :], xsv[:, :], stat[:, 2, i:i + 1], stat[:, 3, i:i + 1],
                    ALU.add, ALU.mult),
                    wait=[(s_sy, xn_free[sl]), (s_v, t_rst)], inc=s_v)
                xs_free[0] = (s_v, t_xn)
                xn_free[sl] = strp(xnT[:, :, 128 * ih:128 * (ih + 1)], xn_t[:, sl, :],
                                   wait=[(s_v, t_xn), (s_t, xnT_busy[ih])])
            t_xnT_half = s_sy.n

            wtoks = {}
            def issue_wq(si):
                wsl = si % 4
                wtoks[si] = gdma(wq_s[:, wsl, :, :],
                                 wqkv_d[128 * si:128 * (si + 1), :]
                                 .rearrange("p (k m) -> p k m", m=128),
                                 wait=[(s_t, wq_busy[wsl])])
            for si in range(4):
                issue_wq(si)
            for si, (kind, fc) in enumerate(CHUNKS):
                wsl = si % 4
                wsem = s_gd
                t_w = wtoks[si]
                psl = si % 2
                PSX = (PS1, PS2)[psl]
                for nsl in range(2):
                    for kc in range(8):
                        tok = b.op("tensor", lambda e, kc=kc, nsl=nsl, wsl=wsl, ih=0,
                                   PSX=PSX:
                                   e.matmul(PSX[:, 512 * nsl:512 * (nsl + 1)],
                                            wq_s[:, wsl, kc, :],
                                            xnT[:, kc, 512 * nsl:512 * (nsl + 1)],
                                            start=(kc == 0), stop=(kc == 7)),
                                   wait=([(wsem, t_w), (s_sy, t_xnT_half),
                                          (s_v, ps_free[psl])]
                                         if (nsl == 0 and kc == 0) else []),
                                   inc=(s_t if (nsl == 1 and kc == 7) else None))
                t_mm = s_t.n
                for ih in range(8):
                    xnT_busy[ih] = t_mm
                wq_busy[wsl] = t_mm
                if si + 4 < len(CHUNKS):
                    issue_wq(si + 4)
                dsth = slice(1024 * h, 1024 * (h + 1))
                if kind == "v":
                    t_v = b.op("vector", lambda e, fc=fc, dsth=dsth, PSX=PSX:
                               e.tensor_copy(v_fm[:, fc, dsth], PSX[:, 0:1024]),
                               wait=[(s_t, t_mm)], inc=s_v)
                    ps_free[psl] = t_v
                    b.op("gpsimd", lambda e, fc=fc, dsth=dsth: e.tensor_tensor(
                        kv_fm[:, fc, dsth], k_fm[:, fc, dsth], v_fm[:, fc, dsth],
                        ALU.mult), wait=[(s_v, t_v), (s_v, tkp[fc])], inc=s_p)
                else:
                    fsl = si % 2
                    t_r = b.op("scalar", lambda e, fsl=fsl, PSX=PSX: e.activation(
                        phi_f[:, fsl, :], PSX[:, 0:1024], AF.Relu, scale=-1.0),
                        wait=[(s_t, t_mm)], inc=s_a)
                    t_e = b.op("scalar", lambda e, fsl=fsl: e.activation(
                        phi_e[:, 2 + fsl, :], phi_f[:, fsl, :], AF.Exp, scale=-1.0),
                        wait=[(s_v, phi_busy[fsl])], inc=s_a)
                    dst = qp_fm if kind == "q" else k_fm
                    t_p = b.op("vector", lambda e, fc=fc, fsl=fsl, dst=dst, dsth=dsth,
                               PSX=PSX: e.scalar_tensor_tensor(
                                   dst[:, fc, dsth], PSX[:, 0:1024], 1.0,
                                   phi_e[:, 2 + fsl, :], ALU.add, ALU.max),
                               wait=[(s_a, t_e)], inc=s_v)
                    ps_free[psl] = t_p
                    phi_busy[fsl] = t_p
                    if kind == "k":
                        tkp[fc] = t_p

        b.barrier(s_ph, DMAS)

        iota_f = A8.bitcast(F32)
        sinrow_f = A9.bitcast(F32)
        agv = A9[0:8, 0:T]
        P1buf = A9.rearrange("p (r t) -> p r t", t=T)
        prodb = A8.rearrange("p (r t) -> p r t", t=T)
        denf = A9.bitcast(F32)

        t_iota = sdma(iota_f[:, :], iota_d[:, :])

        scan_tok = []
        for c in range(8):
            b.op("vector", lambda e, c=c: e.tensor_tensor_scan(
                kv_fm[:, c, :], dpar_sb[:, c:c + 1].broadcast_to([P, T]), kv_fm[:, c, :],
                0.0, ALU.mult, ALU.add), inc=s_v)
            scan_tok.append(b.op("vector", lambda e, c=c: e.tensor_tensor_scan(
                k_fm[:, c, :], dpar_sb[:, c:c + 1].broadcast_to([P, T]), k_fm[:, c, :],
                0.0, ALU.mult, ALU.add), inc=s_v))

        vop(lambda e: e.tensor_copy(
            sout_sb.rearrange("p (c o) -> p c o", o=1)[:, 0:8, :], kv_fm[:, :, T - 1:T]),
            wait=[(s_v, scan_tok[-1])])
        t_so = vop(lambda e: e.tensor_copy(
            sout_sb.rearrange("p (c o) -> p c o", o=1)[:, 8:16, :], k_fm[:, :, T - 1:T]),
            inc=s_v)
        t_sup = sdma(cc_s_in[:, :], sout_sb[:, :], wait=[(s_v, t_so)])
        t_sag = b.op("gpsimd", lambda e: e.collective_compute(
            "AllGather", ALU.bypass, replica_groups=[list(range(NCORE))],
            ins=[cc_s_in.ap().opt()], outs=[cc_s_out.ap().opt()]),
            wait=[(s_sy, t_sup)], inc=s_cc)
        t_agr = gdma(agv, cc_s_out.rearrange("(r p) c -> r (p c)", p=128),
                     wait=[(s_cc, t_sag)])
        t_selmm = None
        for nsl in range(4):
            t_selmm = b.op("tensor", lambda e, nsl=nsl: e.matmul(
                PS2[0:1, 512 * nsl:512 * (nsl + 1)], sel_sb[:, :],
                agv[:, 512 * nsl:512 * (nsl + 1)], start=True, stop=True),
                wait=[(s_gd, t_agr)] if nsl == 0 else [], inc=(s_t if nsl == 3 else None))
        t_sinrow = b.op("scalar", lambda e: e.copy(sinrow_f[0:1, :], PS2[0:1, :]),
                        wait=[(s_t, t_selmm)], inc=s_a)
        t_sio = sdma(sin_dram[:, :], sinrow_f[0:1, :], wait=[(s_a, t_sinrow)])
        t_sin = sdma(sinfm_sb[:, :], sin_dram.rearrange("o (p c) -> (o p) c", c=16),
                     wait=[(s_sy, t_sio)])

        t_corr = [0] * 8
        t_p1_last = 0
        for c in range(8):
            pb = c % 2
            w0 = [(s_sy, t_sin), (s_sy, t_iota)] if c == 0 else []
            if c >= 2:
                w0 = [(s_v, t_corr[c - 2])]
            t_p1 = b.op("scalar", lambda e, c=c, pb=pb: e.activation(
                P1buf[:, pb, :], iota_f[:, :], AF.Exp, scale=lnd_sb[:, c:c + 1]),
                wait=w0, inc=s_a)
            t_p1_last = t_p1
            b.op("vector", lambda e, c=c, pb=pb: e.scalar_tensor_tensor(
                kv_fm[:, c, :], P1buf[:, pb, :], sinfm_sb[:, c:c + 1], kv_fm[:, c, :],
                ALU.mult, ALU.add), wait=[(s_a, t_p1), (s_v, scan_tok[-1])], inc=s_v)
            t_corr[c] = b.op("vector", lambda e, c=c, pb=pb: e.scalar_tensor_tensor(
                k_fm[:, c, :], P1buf[:, pb, :], sinfm_sb[:, 8 + c:9 + c], k_fm[:, c, :],
                ALU.mult, ALU.add), inc=s_v)

        t_pr = [0] * 8
        t_dmm = None
        for c in range(8):
            pb = c % 2
            t_pr[c] = b.op("gpsimd", lambda e, c=c, pb=pb: e.tensor_tensor(
                prodb[:, pb, :], qp_fm[:, c, :], k_fm[:, c, :], ALU.mult),
                wait=[(s_v, t_corr[c]), (s_a, t_p1_last)] +
                     ([(s_t, t_dmm)] if c >= 2 else []), inc=s_p)
            for nsl in range(4):
                t_dmm = b.op("tensor", lambda e, c=c, pb=pb, nsl=nsl: e.matmul(
                    PS2[0:16, 512 * nsl:512 * (nsl + 1)], hs16_sb[:, c, :],
                    prodb[:, pb, 512 * nsl:512 * (nsl + 1)],
                    start=(c == 0), stop=(c == 7)),
                    wait=[(s_p, t_pr[c]), (s_a, t_sinrow)] if nsl == 0 else [],
                    inc=(s_t if nsl == 3 else None))
        t_dene = b.op("vector", lambda e: e.tensor_scalar_max(
            denf[0:16, :], PS2[0:16, :], 1e-6),
            wait=[(s_t, t_dmm), (s_v, t_corr[7])], inc=s_v)
        b.op("vector", lambda e: e.reciprocal(denf[0:16, :], denf[0:16, :]), inc=s_v)
        t_recip = b.op("vector", lambda e: e.tensor_copy(recipb_sb[:, :], denf[0:16, :]),
                       inc=s_v)

        rb_free = [0, 0, 0, 0]
        t_outc = [0] * 8
        last_num = 0
        for c in range(8):
            pb = c % 2
            t_num = b.op("gpsimd", lambda e, c=c, pb=pb: e.tensor_tensor(
                prodb[:, pb, :], qp_fm[:, c, :], kv_fm[:, c, :], ALU.mult),
                wait=[(s_v, t_recip)] + ([(s_v, t_outc[c - 2])] if c >= 2 else []),
                inc=s_p)
            for nsl in range(4):
                t_rb = b.op("tensor", lambda e, c=c, nsl=nsl: e.matmul(
                    PS1[:, 512 * nsl:512 * (nsl + 1)], hs16t_sb[:, c, :],
                    recipb_sb[:, 512 * nsl:512 * (nsl + 1)], start=True, stop=True),
                    wait=[(s_p, t_num), (s_v, rb_free[nsl])] if nsl == 0
                    else [(s_v, rb_free[nsl])],
                    inc=s_t)
                tok = b.op("vector", lambda e, c=c, pb=pb, nsl=nsl: e.tensor_tensor(
                    kv_fm[:, c, 512 * nsl:512 * (nsl + 1)],
                    prodb[:, pb, 512 * nsl:512 * (nsl + 1)],
                    PS1[:, 512 * nsl:512 * (nsl + 1)], ALU.mult),
                    wait=[(s_t, t_rb)], inc=s_v)
                rb_free[nsl] = tok
            t_outc[c] = tok

        dbg("out_fm", kv_fm[:, :, :])
        dbg("runk_fm", k_fm[:, :, :])
        dbg("recipb", recipb_sb[:, :])
        dbg("sout", sout_sb[:, :])
        dbg("sinfm", sinfm_sb[:, :])
        dbg("dpar", dpar_sb[:, :])
        dbg("mq", mq_sb[:, :])
        b.barrier(s_ph, DMAS)

        wg_sb = A6.rearrange("p (k n) -> p k n", n=1024)
        wo_sb = A7.rearrange("p (k n) -> p k n", n=1024)
        g_fm = k_fm
        mix_fm = kv_fm
        af_t = A8.rearrange("p (r d) -> p r d", d=D)
        dbuf = A9.rearrange("p (r t) -> p r t", t=T)

        t_wg = gdma(wg_sb[:, :, :], wgate_d.rearrange("(k p) n -> p k n", p=128))
        t_wo = gdma(wo_sb[:, :, :], wout_d.rearrange("(k p) n -> p k n", p=128))

        t_gev = [0] * 8
        for oc in range(8):
            ps = (PS1, PS2)[oc % 2]
            mmtok = None
            for nsl in range(4):
                for kc in range(8):
                    mmtok = b.op("tensor", lambda e, oc=oc, nsl=nsl, kc=kc, ps=ps:
                                 e.matmul(ps[:, 512 * nsl:512 * (nsl + 1)],
                                          wg_sb[:, kc, 128 * oc:128 * (oc + 1)],
                                          kv_fm[:, kc, 512 * nsl:512 * (nsl + 1)],
                                          start=(kc == 0), stop=(kc == 7)),
                                 wait=([(s_gd, t_wg), (s_a, t_gev[oc - 2] if oc >= 2 else 0)]
                                       if (nsl == 0 and kc == 0) else []),
                                 inc=(s_t if (nsl == 3 and kc == 7) else None))
            t_gev[oc] = b.op("scalar", lambda e, oc=oc, ps=ps: e.activation(
                g_fm[:, oc, :], ps[:, :], AF.Sigmoid, bias=bgfm_sb[:, oc:oc + 1]),
                wait=[(s_t, mmtok)], inc=s_a)
        gate_mm_all = mmtok
        t_mx = [0] * 8
        for oc in range(8):
            pb = oc % 2
            b.op("gpsimd", lambda e, oc=oc, pb=pb: e.tensor_tensor(
                dbuf[:, pb, :], kv_fm[:, oc, :], v_fm[:, oc, :], ALU.subtract),
                wait=[(s_a, t_gev[oc]), (s_t, gate_mm_all)] +
                     ([(s_p, t_mx[oc - 2])] if oc >= 2 else []),
                inc=s_p)
            b.op("gpsimd", lambda e, oc=oc, pb=pb: e.tensor_tensor(
                dbuf[:, pb, :], dbuf[:, pb, :], g_fm[:, oc, :], ALU.mult), inc=s_p)
            t_mx[oc] = b.op("gpsimd", lambda e, oc=oc, pb=pb: e.tensor_tensor(
                mix_fm[:, oc, :], dbuf[:, pb, :], v_fm[:, oc, :], ALU.add), inc=s_p)
        t_mix_all = t_mx[7]
        t_afs = [0] * NT
        for i in range(NT):
            ps = (PS1, PS2)[i % 2]
            mmtok = None
            for nsl in range(2):
                for kc in range(8):
                    mmtok = b.op("tensor", lambda e, i=i, nsl=nsl, kc=kc, ps=ps:
                                 e.matmul(ps[:, 512 * nsl:512 * (nsl + 1)],
                                          mix_fm[:, kc, 128 * i:128 * (i + 1)],
                                          wo_sb[:, kc, 512 * nsl:512 * (nsl + 1)],
                                          start=(kc == 0), stop=(kc == 7)),
                                 wait=([(s_p, t_mix_all), (s_gd, t_wo),
                                        (s_sy, t_afs[i - 2] if i >= 2 else 0)]
                                       if (nsl == 0 and kc == 0) else []),
                                 inc=(s_t if (nsl == 1 and kc == 7) else None))
            t_ev = b.op("scalar", lambda e, i=i, ps=ps: e.activation(
                af_t[:, i % 2, :], ps[:, 0:1024], AF.Copy), wait=[(s_t, mmtok)], inc=s_a)
            t_afs[i] = sdma(af_dram.rearrange("(i p) d -> p i d", p=128)[:, i, :],
                            af_t[:, i % 2, :], wait=[(s_a, t_ev)])

        dbg("mix_fm", mix_fm[:, :, :])
        dbg("g_fm", g_fm[:, :, :])
        b.barrier(s_ph, DMAS)

        g1b = A1[:, 0:1024]
        b1lb = A1[:, 1024:2048]
        dqx_fm = A1[:, 2048:6144].rearrange("p (k t) -> p k t", t=512)
        eslot = A1.rearrange("p (r d) -> p r d", d=D)
        g2b = A6[:, 0:4096]
        b2lb = A6[:, 4096:8192]
        w2lo = A4.rearrange("p (k n) -> p k n", n=1024)
        w2hi = A5.rearrange("p (k n) -> p k n", n=1024)
        h2tm = A3.rearrange("p (i f) -> p i f", f=FF)
        w1s = A7[:, 0:4096].rearrange("p (r k q) -> p r k q", r=4, k=8)
        dqx2_t = A7[:, 4096:8192]
        afr_t = A8[:, 3072:4096]
        a2f = A2.bitcast(F32)
        ln1b = A2.rearrange("p (r d) -> p r d", d=D)
        ln2b = A2[:, 2048:10240].rearrange("p (r f) -> p r f", f=FF)
        ybuf = a2f.rearrange("p (r d) -> p r d", d=D)
        xq1g = A2.bitcast(I8)[:, 28672:32768].rearrange("p (j d) -> p j d", d=D)
        xq2_t = A8.bitcast(I8)[:, 0:4096]
        h2f_t = A8.rearrange("p (r q) -> p r q", q=512)
        dqx2fm = A9.rearrange("p (k q) -> p k q", q=128)

        gdma(g1b[:, :], g1row_d[:, :].broadcast_to([P, D]))
        gdma(b1lb[:, :], b1lrow_d[:, :].broadcast_to([P, D]))
        gdma(g2b[:, :], g2row_d[:, :].broadcast_to([P, FF]))
        t_bct = gdma(b2lb[:, :], b2lrow_d[:, :].broadcast_to([P, FF]))
        cdma(w2lo[:, :, :], cc_w2_out[0:2048, :].rearrange("(k p) n -> p k n", p=128),
             wait=[(s_cc, t_ag2)])
        t_w2 = cdma(w2hi[:, :, :], cc_w2_out[2048:4096, :].rearrange("(k p) n -> p k n",
                                                                     p=128))

        st = {
            "e_free": [0, 0],
            "ln1f_free": [0, 0],
            "ln2f_free": [0, 0],
            "dqx2t_free": 0,
            "dqxT_busy": 0,
            "dqx_grp": 0,
            "h2_busy": 0,
            "w1s_busy": [0, 0, 0, 0],
            "ps1_free": [0, 0, 0, 0],
            "h2f_free": [0, 0],
            "d2lo_busy": 0,
            "d2hi_busy": 0,
            "ps2h_free": [0, 0],
            "ybuf_free": [0, 0],
            "afr_free": 0,
            "h2grp": 0,
            "d2t": {},
            "t_af": {},
            "t_xh": {},
        }

        def emit_E_pair(g, j0):
            JJ = (j0, j0 + 1)
            tk = {}
            for j in JJ:
                i = g * GRP + j
                sl = j % 2
                t_x = sdma(xsv[:, :],
                           x_in.rearrange("(i p) d -> p i d", p=128)[:, i, :],
                           wait=[xs_free[0]] if xs_free[0][0] else [])
                t_xn = b.op("vector", lambda e, i=i, sl=sl: e.tensor_scalar(
                    eslot[:, 6 + sl, :], xsv[:, :], stat[:, 2, i:i + 1],
                    stat[:, 3, i:i + 1], ALU.add, ALU.mult),
                    wait=[(s_sy, t_x), (s_sy, st["e_free"][sl])], inc=s_v)
                xs_free[0] = (s_v, t_xn)
                tk[("xn", j)] = t_xn
            for j in JJ:
                i = g * GRP + j
                sl = j % 2
                vop(lambda e, i=i, sl=sl: e.reduce_sum(
                    stat[:, 0, i:i + 1], eslot[:, 6 + sl, :], AX.X),
                    wait=[(s_v, tk[("xn", j)])])
            for j in JJ:
                i = g * GRP + j
                sl = j % 2
                tk[("sq", j)] = b.op("scalar", lambda e, i=i, sl=sl: e.activation(
                    ln1b[:, sl, :], eslot[:, 6 + sl, :], AF.Square,
                    accum_out=stat[:, 1, i:i + 1]),
                    wait=[(s_v, tk[("xn", j)]), (s_v, st["ln1f_free"][sl])], inc=s_a)
            for j in JJ:
                i = g * GRP + j
                tk[("neg", j)] = vop(lambda e, i=i: e.tensor_scalar_mul(
                    stat[:, 4, i:i + 1], stat[:, 0, i:i + 1], -INV_D),
                    wait=[(s_a, tk[("sq", j)])], inc=s_v)
            for j in JJ:
                i = g * GRP + j
                tk[("m2", j)] = vop(lambda e, i=i: e.tensor_scalar_mul(
                    stat[:, 6, i:i + 1], stat[:, 4, i:i + 1], stat[:, 4, i:i + 1]),
                    wait=[(s_v, tk[("neg", j)])], inc=s_v)
            for j in JJ:
                i = g * GRP + j
                vop(lambda e, i=i: e.tensor_scalar(
                    stat[:, 5, i:i + 1], stat[:, 1, i:i + 1], INV_D, float(LN_EPS),
                    ALU.mult, ALU.add))
            for j in JJ:
                i = g * GRP + j
                tk[("v1", j)] = vop(lambda e, i=i: e.tensor_tensor(
                    stat[:, 5, i:i + 1], stat[:, 5, i:i + 1], stat[:, 6, i:i + 1],
                    ALU.subtract), wait=[(s_v, tk[("m2", j)])], inc=s_v)
            for j in JJ:
                i = g * GRP + j
                tk[("s1", j)] = aop(lambda e, i=i: e.activation(
                    stat[:, 5, i:i + 1], stat[:, 5, i:i + 1], AF.Sqrt),
                    wait=[(s_v, tk[("v1", j)])], inc=s_a)
            for j in JJ:
                i = g * GRP + j
                tk[("rs", j)] = vop(lambda e, i=i: e.reciprocal(
                    stat[:, 5, i:i + 1], stat[:, 5, i:i + 1]),
                    wait=[(s_a, tk[("s1", j)])], inc=s_v)
            for j in JJ:
                i = g * GRP + j
                sl = j % 2
                b.op("vector", lambda e, i=i, sl=sl: e.tensor_scalar_mul(
                    ln1b[:, sl, :], g1b[:, :], stat[:, 5, i:i + 1]),
                    wait=[(s_v, tk[("rs", j)]), (s_gd, t_bct)], inc=s_v)
            for j in JJ:
                i = g * GRP + j
                sl = j % 2
                b.op("vector", lambda e, i=i, sl=sl: e.scalar_tensor_tensor(
                    ln1b[:, sl, :], eslot[:, 6 + sl, :], stat[:, 4, i:i + 1],
                    ln1b[:, sl, :], ALU.add, ALU.mult), inc=s_v)
            for j in JJ:
                sl = j % 2
                b.op("vector", lambda e, sl=sl: e.tensor_tensor(
                    ln1b[:, sl, :], ln1b[:, sl, :], b1lb[:, :], ALU.add), inc=s_v)
            for j in JJ:
                i = g * GRP + j
                sl = j % 2
                vop(lambda e, i=i, sl=sl: e.reduce_max(
                    stat[:, 6, i:i + 1], ln1b[:, sl, :], AX.X,
                    apply_absolute_value=True))
            for j in JJ:
                i = g * GRP + j
                vop(lambda e, i=i: e.tensor_scalar_max(
                    stat[:, 6, i:i + 1], stat[:, 6, i:i + 1], 1e-5))
            for j in JJ:
                i = g * GRP + j
                vop(lambda e, i=i: e.reciprocal(
                    stat[:, 7, i:i + 1], stat[:, 6, i:i + 1]))
            for j in JJ:
                i = g * GRP + j
                tk[("s127", j)] = vop(lambda e, i=i: e.tensor_scalar_mul(
                    stat[:, 7, i:i + 1], stat[:, 7, i:i + 1], 127.0), inc=s_v)
            for j in JJ:
                i = g * GRP + j
                sl = j % 2
                b.op("vector", lambda e, i=i, j=j, sl=sl: e.tensor_scalar_mul(
                    xq1g[:, j, :], ln1b[:, sl, :], stat[:, 7, i:i + 1]),
                    wait=[(s_v, tk[("s127", j)])], inc=s_v)
            for j in JJ:
                i = g * GRP + j
                tk[("rs1", j)] = vop(lambda e, i=i: e.tensor_scalar_mul(
                    stat[:, 6, i:i + 1], stat[:, 6, i:i + 1], 1.0 / 127.0), inc=s_v)
            for j in JJ:
                i = g * GRP + j
                sl = j % 2
                t_dq = b.op("vector", lambda e, i=i, j=j, sl=sl: e.tensor_scalar_mul(
                    eslot[:, 6 + sl, :], xq1g[:, j, :], stat[:, 6, i:i + 1]),
                    wait=[(s_v, tk[("rs1", j)])], inc=s_v)
                st["ln1f_free"][sl] = t_dq
                st["e_free"][sl] = strp(
                    dqx_fm[:, :, 128 * j:128 * (j + 1)], eslot[:, 6 + sl, :],
                    wait=[(s_v, t_dq), (s_t, st["dqxT_busy"])])
            if j0 + 2 == GRP:
                st["dqx_grp"] = s_sy.n

        def emit_F(g):
            toks = {}
            def issue_slab(ofc):
                wsl = ofc % 4
                toks[ofc] = cdma(w1s[:, wsl, :, :],
                                 cc_w1_out[128 * ofc:128 * (ofc + 1), :]
                                 .rearrange("p (k m) -> p k m", m=128),
                                 wait=[(s_cc, t_ag1), (s_t, st["w1s_busy"][wsl])])
            for ofc in range(4):
                issue_slab(ofc)
            last_mm = None
            for ofc in range(NC2):
                wsl = ofc % 4
                psl = ofc % 4
                ps = PS1[:, 512 * psl:512 * (psl + 1)]
                for kc in range(8):
                    last_mm = b.op("tensor", lambda e, kc=kc, wsl=wsl, ps=ps:
                                   e.matmul(ps, w1s[:, wsl, kc, :],
                                            dqx_fm[:, kc, :],
                                            start=(kc == 0), stop=(kc == 7)),
                                   wait=([(s_sc, toks[ofc]), (s_sy, st["dqx_grp"]),
                                          (s_a, st["ps1_free"][psl])] if kc == 0 else []),
                                   inc=(s_t if kc == 7 else None))
                st["w1s_busy"][wsl] = last_mm
                if ofc + 4 < NC2:
                    issue_slab(ofc + 4)
                hsl = ofc % 2
                t_gelu = b.op("scalar", lambda e, ofc=ofc, ps=ps, hsl=hsl: e.activation(
                    h2f_t[:, 4 + hsl, :], ps, AF.Gelu,
                    bias=b1fm_sb[:, ofc:ofc + 1], scale=mq_sb[:, 0:1]),
                    wait=[(s_t, last_mm), (s_sy, st["h2f_free"][hsl])], inc=s_a)
                st["ps1_free"][psl] = t_gelu
                st["h2f_free"][hsl] = strp(
                    h2tm[:, :, 128 * ofc:128 * (ofc + 1)],
                    h2f_t[:, 4 + hsl, :],
                    wait=[(s_a, t_gelu), (s_v, st["h2_busy"])])
            st["dqxT_busy"] = last_mm

        def emit_G_pair(i0):
            II = (i0, i0 + 1)
            t_h2g = st["h2grp"]
            tok = {}
            for i in II:
                vop(lambda e, i=i: e.reduce_sum(
                    st2[:, 0, i:i + 1], h2tm[:, i % GRP, :], AX.X),
                    wait=[(s_sy, t_h2g)])
            for i in II:
                sl = i % 2
                tok[("sq", i)] = b.op("scalar", lambda e, i=i, sl=sl: e.activation(
                    ln2b[:, sl, :], h2tm[:, i % GRP, :], AF.Square,
                    accum_out=st2[:, 1, i:i + 1]),
                    wait=[(s_sy, t_h2g), (s_v, st["ln2f_free"][sl])], inc=s_a)
            for i in II:
                tok[("neg", i)] = vop(lambda e, i=i: e.tensor_scalar_mul(
                    st2[:, 2, i:i + 1], st2[:, 0, i:i + 1], -INV_FF),
                    wait=[(s_a, tok[("sq", i)])], inc=s_v)
            for i in II:
                tok[("m2", i)] = vop(lambda e, i=i: e.tensor_scalar_mul(
                    st2[:, 4, i:i + 1], st2[:, 2, i:i + 1], st2[:, 2, i:i + 1]),
                    wait=[(s_v, tok[("neg", i)])], inc=s_v)
            for i in II:
                vop(lambda e, i=i: e.tensor_scalar(
                    st2[:, 3, i:i + 1], st2[:, 1, i:i + 1], INV_FF, float(LN_EPS),
                    ALU.mult, ALU.add))
            for i in II:
                tok[("var", i)] = vop(lambda e, i=i: e.tensor_tensor(
                    st2[:, 3, i:i + 1], st2[:, 3, i:i + 1], st2[:, 4, i:i + 1],
                    ALU.subtract), wait=[(s_v, tok[("m2", i)])], inc=s_v)
            for i in II:
                tok[("sd", i)] = aop(lambda e, i=i: e.activation(
                    st2[:, 3, i:i + 1], st2[:, 3, i:i + 1], AF.Sqrt),
                    wait=[(s_v, tok[("var", i)])], inc=s_a)
            for i in II:
                tok[("rsd", i)] = vop(lambda e, i=i: e.reciprocal(
                    st2[:, 3, i:i + 1], st2[:, 3, i:i + 1]),
                    wait=[(s_a, tok[("sd", i)])], inc=s_v)
            for i in II:
                sl = i % 2
                b.op("vector", lambda e, i=i, sl=sl: e.tensor_scalar_mul(
                    ln2b[:, sl, :], g2b[:, :], st2[:, 3, i:i + 1]),
                    wait=[(s_v, tok[("rsd", i)]), (s_gd, t_bct)], inc=s_v)
            t_ln2 = None
            for i in II:
                sl = i % 2
                t_ln2 = b.op("vector", lambda e, i=i, sl=sl: e.scalar_tensor_tensor(
                    ln2b[:, sl, :], h2tm[:, i % GRP, :], st2[:, 2, i:i + 1],
                    ln2b[:, sl, :], ALU.add, ALU.mult), inc=s_v)
            st["h2_busy"] = t_ln2
            for i in II:
                sl = i % 2
                tok[("l2", i)] = b.op("vector", lambda e, sl=sl: e.tensor_tensor(
                    ln2b[:, sl, :], ln2b[:, sl, :], b2lb[:, :], ALU.add), inc=s_v)
            for i in II:
                sl = i % 2
                vop(lambda e, i=i, sl=sl: e.reduce_max(
                    st2[:, 4, i:i + 1], ln2b[:, sl, :], AX.X,
                    apply_absolute_value=True))
            for i in II:
                vop(lambda e, i=i: e.tensor_scalar_max(
                    st2[:, 4, i:i + 1], st2[:, 4, i:i + 1], 1e-5))
            for i in II:
                vop(lambda e, i=i: e.reciprocal(
                    st2[:, 5, i:i + 1], st2[:, 4, i:i + 1]))
            for i in II:
                tok[("s2", i)] = vop(lambda e, i=i: e.tensor_scalar_mul(
                    st2[:, 5, i:i + 1], st2[:, 5, i:i + 1], 127.0), inc=s_v)
            for i in II:
                vop(lambda e, i=i: e.tensor_scalar_mul(
                    st2[:, 6, i:i + 1], st2[:, 4, i:i + 1], 1.0 / 127.0))
            for i in II:
                tok[("rs2", i)] = vop(lambda e, i=i: e.tensor_tensor(
                    st2[:, 6, i:i + 1], st2[:, 6, i:i + 1], mq_sb[:, 1:2], ALU.mult),
                    inc=s_v)
            return tok

        def emit_G_tail(i, tok):
            sl = i % 2
            b.op("vector", lambda e, i=i, sl=sl: e.tensor_scalar_mul(
                xq2_t[:, :], ln2b[:, sl, :], st2[:, 5, i:i + 1]),
                wait=[(s_v, tok[("s2", i)])], inc=s_v)
            t_dq2 = b.op("vector", lambda e, i=i: e.tensor_scalar_mul(
                dqx2_t[:, :], xq2_t[:, :], st2[:, 6, i:i + 1]),
                wait=[(s_v, tok[("rs2", i)]), (s_sy, st["dqx2t_free"])], inc=s_v)
            st["ln2f_free"][sl] = t_dq2
            t_lo = strp(dqx2fm[:, 0:16, :], dqx2_t[:, 0:2048],
                        wait=[(s_v, t_dq2), (s_t, st["d2lo_busy"])])
            t_hi = strp(dqx2fm[:, 16:32, :], dqx2_t[:, 2048:4096],
                        wait=[(s_t, st["d2hi_busy"])])
            st["dqx2t_free"] = t_hi
            st["d2t"][i] = (t_lo, t_hi)
            st["t_af"][i] = gdma(afr_t[:, :],
                                 af_dram.rearrange("(i p) d -> p i d", p=128)[:, i, :],
                                 wait=[(s_p, st["afr_free"])])
            st["t_xh"][i] = sdma(xsv[:, :],
                                 x_in.rearrange("(i p) d -> p i d", p=128)[:, i, :],
                                 wait=[xs_free[0]] if xs_free[0][0] else [])

        def emit_H(i):
            psh = i % 2
            t_af = st["t_af"][i]
            t_xh = st["t_xh"][i]
            mmh = None
            t_lo, t_hi = st["d2t"][i]
            for kc in range(32):
                w2v = w2lo if kc < 16 else w2hi
                kcl = kc % 16
                for nsl in range(2):
                    wl = []
                    if kc == 0 and nsl == 0:
                        wl = [(s_sy, t_lo), (s_sc, t_w2),
                              (s_a, st["ps2h_free"][psh])]
                    elif kc == 16 and nsl == 0:
                        wl = [(s_sy, t_hi)]
                    mmh = b.op("tensor",
                               lambda e, kc=kc, kcl=kcl, nsl=nsl, psh=psh, w2v=w2v:
                               e.matmul(PS2[:, 1024 * psh + 512 * nsl:
                                            1024 * psh + 512 * (nsl + 1)],
                                        dqx2fm[:, kc, :],
                                        w2v[:, kcl, 512 * nsl:512 * (nsl + 1)],
                                        start=(kc == 0), stop=False),
                               wait=wl,
                               inc=(s_t if (kc == 15 and nsl == 1) else None))
                    if kc == 15 and nsl == 1:
                        st["d2lo_busy"] = mmh
            for nsl in range(2):
                mmh = b.op("tensor", lambda e, nsl=nsl, psh=psh: e.matmul(
                    PS2[:, 1024 * psh + 512 * nsl:1024 * psh + 512 * (nsl + 1)],
                    ones1[0:1, 0:128], b2row_sb[0:1, 512 * nsl:512 * (nsl + 1)],
                    start=False, stop=True),
                    inc=(s_t if nsl == 1 else None))
            st["d2hi_busy"] = mmh
            t_ycp = b.op("scalar", lambda e, psh=psh: e.activation(
                ybuf[:, 5 + psh, :], PS2[:, 1024 * psh:1024 * psh + 1024], AF.Copy),
                wait=[(s_t, mmh), (s_gd, st["ybuf_free"][psh])], inc=s_a)
            st["ps2h_free"][psh] = t_ycp
            t_ya = b.op("gpsimd", lambda e, psh=psh: e.tensor_tensor(
                ybuf[:, 5 + psh, :], ybuf[:, 5 + psh, :], xsv[:, :], ALU.add),
                wait=[(s_a, t_ycp), (s_sy, t_xh)], inc=s_p)
            xs_free[0] = (s_p, t_ya)
            t_yb = b.op("gpsimd", lambda e, psh=psh: e.tensor_tensor(
                ybuf[:, 5 + psh, :], ybuf[:, 5 + psh, :], afr_t[:, :], ALU.add),
                wait=[(s_gd, t_af)], inc=s_p)
            st["afr_free"] = t_yb
            st["ybuf_free"][psh] = gdma(
                y_out.rearrange("(i p) d -> p i d", p=128)[:, i, :],
                ybuf[:, 5 + psh, :], wait=[(s_p, t_yb)])

        for g in range(NGRP + 1):
            if g < NGRP:
                emit_E_pair(g, 0)
                emit_E_pair(g, 2)
            if g >= 1:
                i0 = GRP * (g - 1)
                gt0 = emit_G_pair(i0)
                emit_G_tail(i0, gt0)
                emit_H(i0)
                emit_G_tail(i0 + 1, gt0)
                gt1 = emit_G_pair(i0 + 2)
                emit_H(i0 + 1)
                emit_G_tail(i0 + 2, gt1)
                emit_H(i0 + 2)
                emit_G_tail(i0 + 3, gt1)
                emit_H(i0 + 3)
            if g < NGRP:
                emit_F(g)
                st["h2grp"] = s_sy.n


        b.op("sync", lambda e: e.nop(), wait=[(s_sy, s_sy.n)])
        b.op("scalar", lambda e: e.nop(), wait=[(s_sc, s_sc.n)])
        b.op("gpsimd", lambda e: e.nop(), wait=[(s_gd, s_gd.n)])
        b.flush()
    return nc


_NC_CACHE = {}


def _host_prepare(inputs):
    f32 = lambda a: np.ascontiguousarray(np.asarray(a), dtype=np.float32)
    x = f32(inputs["L0_attn_x"])
    Wqkv = f32(inputs["L0_attn_Wqkv"])
    Wout = f32(inputs["L0_attn_Wout"])
    Wgate = f32(inputs["L0_attn_Wgate"])
    bgate = f32(inputs["L0_attn_bgate"])
    dp = f32(inputs["L0_attn_decay_param"])
    g1 = f32(inputs["L0_ffn_ln1_g"])
    b1l = f32(inputs["L0_ffn_ln1_b"])
    w1 = f32(inputs["L0_ffn_w1"])
    b1 = f32(inputs["L0_ffn_b1"])
    g2 = f32(inputs["L0_ffn_ln2_g"])
    b2l = f32(inputs["L0_ffn_ln2_b"])
    w2 = f32(inputs["L0_ffn_w2"])
    b2 = f32(inputs["L0_ffn_b2"])

    order = list(range(8)) + [x for pair in ((8 + c, 16 + c) for c in range(8))
                              for x in pair]
    wq = Wqkv.reshape(24, 128, 1024)
    slabs = []
    for oc in order:
        blk = wq[oc]
        sl = blk.reshape(128, 8, 128).transpose(2, 1, 0)
        slabs.append(sl.reshape(128, 1024))
    wqkv_cols = np.ascontiguousarray(np.concatenate(slabs, axis=0))
    wgateT = np.ascontiguousarray(Wgate.T)
    woutT = np.ascontiguousarray(Wout.T)
    w2T = np.ascontiguousarray(w2.T)

    bgate_fm = np.ascontiguousarray(bgate.reshape(8, 128).T)
    b1_fm = np.ascontiguousarray(b1.reshape(32, 128).T)
    idx = (np.arange(128)[:, None] // 64) + 2 * np.arange(8)[None, :]
    param_fm = np.ascontiguousarray(dp[idx])
    iota_bc = np.ascontiguousarray(
        np.broadcast_to(np.arange(1, T + 1, dtype=np.float32), (P, T)))
    fc = np.arange(8)[None, :, None]
    p_ = np.arange(128)[:, None, None]
    m_ = np.arange(16)[None, None, :]
    hs16 = (2 * fc + p_ // 64 == m_).astype(np.float32).reshape(128, 128)
    m2_ = np.arange(16)[:, None, None]
    fc2 = np.arange(8)[None, :, None]
    p2_ = np.arange(128)[None, None, :]
    hs16t = (2 * fc2 + p2_ // 64 == m2_).astype(np.float32).reshape(16, 1024)

    shared = {
        "wqkv_cols": wqkv_cols, "wgateT": wgateT, "woutT": woutT,
        "bgate_fm": bgate_fm, "b1_fm": b1_fm,
        "b2_row": b2.reshape(1, D), "g1_row": g1.reshape(1, D),
        "b1l_row": b1l.reshape(1, D), "g2_row": g2.reshape(1, FF),
        "b2l_row": b2l.reshape(1, FF), "param_fm": param_fm,
        "iota_bc": iota_bc, "hs16_in": hs16, "hs16t_in": hs16t,
    }
    in_maps = []
    for c in range(NCORE):
        bseq, half = c // 2, c % 2
        sel = np.zeros((8, 1), dtype=np.float32)
        if half == 1:
            sel[c - 1, 0] = 1.0
        m = dict(shared)
        m["x_in"] = np.ascontiguousarray(x[bseq, half * T:(half + 1) * T])
        m["w1_shard"] = np.ascontiguousarray(w1[512 * c:512 * (c + 1), :])
        m["w2t_shard"] = np.ascontiguousarray(w2T[512 * c:512 * (c + 1), :])
        m["sel8"] = sel
        in_maps.append(m)
    return in_maps


def kernel(**inputs):
    if "nc" not in _NC_CACHE:
        _NC_CACHE["nc"] = build_nc()
    nc = _NC_CACHE["nc"]
    in_maps = _host_prepare(inputs)
    res = run_bass_kernel_spmd(nc, in_maps, core_ids=list(range(NCORE)))
    out = np.zeros((4, 4096, 1024), dtype=np.float32)
    for c in range(NCORE):
        bseq, half = c // 2, c % 2
        out[bseq, half * T:(half + 1) * T] = np.asarray(res.results[c]["y_out"])
    return out
